# version 07 (final)
# speedup vs baseline: 1.1432x; 1.0022x over previous
import math
import numpy as np
import ml_dtypes

import concourse.bass as bass
import concourse.mybir as mybir
import bass_rust
from concourse.bass_utils import run_bass_kernel_spmd
from concourse.aot_env import allow_ici_all_to_all

F32 = mybir.dt.float32
BF16 = mybir.dt.bfloat16
I8 = mybir.dt.int8
I32 = mybir.dt.int32
AF = mybir.ActivationFunctionType
ALU = mybir.AluOpType
AX = mybir.AxisListType

NCORE = 8
B = 2
T = 8192
D = 1024
TC = 2048
NT = TC // 128
ND = D // 128
EPS = 1e-5
BIGNEG = -1e30
LNSCALE = math.log(32.0 ** -0.5)
MAGIC = 12582912.0

_STAGE = 5
_SUB = 9
_CP = 6



def _prep_shared(inp):
    c = np.ascontiguousarray
    w = {}
    w["ln_w8"] = c(inp["L0_attn_ln_w"].reshape(8, 128).T)
    w["ln_b8"] = c(inp["L0_attn_ln_b"].reshape(8, 128).T)
    w["Wq"] = c(inp["L0_attn_Wq"])
    w["Wk"] = c(inp["L0_attn_Wk"])
    w["Wv"] = c(inp["L0_attn_Wv"])
    w["gk_w1"] = c(inp["L0_attn_gk_w1"])
    w["gk_w2"] = c(inp["L0_attn_gk_w2"])
    w["gkb2"] = c(inp["L0_attn_gk_b2"][None, :])
    w["Wg"] = c(inp["L0_attn_Wg"])
    w["rmsw_tp"] = c(np.tile(inp["L0_attn_rms_w"][None, :], (128, 1)))
    w["Wo"] = c(inp["L0_attn_Wo"])
    w["W1"] = c(inp["L0_attn_W1"])
    w["b1t"] = c(inp["L0_attn_b1"].reshape(16, 128).T)
    w["W2"] = c(inp["L0_attn_W2"])
    w["b2t"] = c(inp["L0_attn_b2"].reshape(8, 128).T)
    w["f_w1T"] = c(inp["L0_ffn_w1"].T)
    w["f_w2T"] = c(inp["L0_ffn_w2"].T)
    cw = inp["L0_ffn_conv_w"][:, 0, :]
    w["convw"] = c(np.stack([cw[:, j].reshape(32, 128).T for j in range(3)],
                            axis=2).reshape(128, 96))
    w["convw"] = c(np.einsum("pij->pij", np.stack(
        [cw[:, j].reshape(32, 128).T for j in range(3)], axis=2)).reshape(128, 96))
    w["convb"] = c(inp["L0_ffn_conv_b"].reshape(32, 128).T)
    w["a_wq"] = c(inp["L1_attn_wq"])
    w["a_wkd"] = c(inp["L1_attn_wk_down"])
    w["a_wvd"] = c(inp["L1_attn_wv_down"])
    w["a_wku"] = c(inp["L1_attn_wk_up"])
    w["a_wvu"] = c(inp["L1_attn_wv_up"])
    w["a_wo"] = c(inp["L1_attn_wo"])
    w["e_Wc"] = c(inp["L1_ffn_Wc"].reshape(1024, 128))
    w["e_bc"] = c(inp["L1_ffn_bc"].reshape(128, 1))
    w["e_W1f"] = c(inp["L1_ffn_W1"].transpose(2, 0, 1, 3).reshape(1024, 4096))
    w["e_b1f"] = c(inp["L1_ffn_b1"].reshape(4096).reshape(32, 128).T)
    w["e_W2f"] = c(inp["L1_ffn_W2"].transpose(0, 1, 3, 2).reshape(4096, 1024))
    w["e_b2f"] = c(inp["L1_ffn_b2"].reshape(128, 1024))
    ms = np.zeros((128, 32, 128), np.float32)
    for mt in range(32):
        for m in range(128):
            ms[4 * mt + m // 32, mt, m] = BIGNEG
    w["e_msel"] = c(ms.reshape(128, 4096).astype(ml_dtypes.bfloat16))
    return w


SCAL_VALS = [EPS, LNSCALE, 1.0, 0.0, 0.0, 0.0, 0.0, 0.0]


def _prep_consts():
    c = np.ascontiguousarray
    k = {}
    k["scals"] = np.tile(np.array([SCAL_VALS], np.float32), (128, 1))
    k["identb"] = c(np.eye(128, dtype=ml_dtypes.bfloat16))
    k["identf"] = c(np.eye(128, dtype=np.float32))
    tri = np.tril(np.ones((128, 128), np.float32)).T * (-1.0 / 16.0)
    k["triu16"] = c(tri)
    tie = np.linspace(0.0, 1e-6, 32).astype(np.float32)
    k["tie128"] = c(np.tile(tie[None, :], (128, 1)))
    m = (np.arange(512)[:, None] <= np.arange(512)[None, :]).astype(ml_dtypes.bfloat16)
    k["trilm4"] = c(np.concatenate([m[128 * j:128 * j + 128, :] for j in range(4)], axis=1))
    R = np.zeros((4, 128), np.float32)
    for a in range(4):
        R[a, 32 * a:32 * a + 32] = 1.0
    k["repl4"] = c(R)
    k["ones1"] = np.ones((1, 128), np.float32)
    k["onesf"] = np.ones((128, 128), np.float32)
    k["onescol"] = np.ones((128, 1), np.float32)
    k["onescol_b"] = np.ones((128, 1), ml_dtypes.bfloat16)
    k["trilf"] = c((np.arange(128)[:, None] <= np.arange(128)[None, :]).astype(np.float32))
    return k


CSTF_MAP = {}
CSTB_MAP = {}

def _mk_maps():
    off = 0
    for name, rows, cols in [
        ("onescol", 128, 1), ("identf", 128, 128), ("triu16", 128, 128),
        ("trilf", 128, 128), ("ln_w8", 128, 8), ("ln_b8", 128, 8),
        ("rmsw_tp", 128, 64), ("glaD", 128, 3), ("glaDinv", 128, 3), ("glaS", 128, 3),
        ("attn_mb", 128, 4), ("repl4", 4, 128), ("ones1", 1, 128),
        ("b1t", 128, 16), ("b2t", 128, 8), ("e_bc", 128, 1), ("e_b1f", 128, 32),
        ("tie128", 128, 32), ("convw", 128, 96), ("convb", 128, 32),
        ("scals", 128, 8), ("gkb2", 1, 512), ("selL", 128, 4), ("selR", 128, 4),
        ("onesf", 128, 128),
    ]:
        CSTF_MAP[name] = (off, rows, cols)
        off += cols
    CSTF_MAP["__total__"] = off
    off = 0
    for name, rows, cols in [
        ("identb", 128, 128), ("onescol_b", 128, 1), ("trilm4", 128, 2048),
    ]:
        CSTB_MAP[name] = (off, rows, cols)
        off += cols
    CSTB_MAP["__total__"] = off

_mk_maps()


def _pack_consts(vals):
    nf = CSTF_MAP["__total__"]
    nb = CSTB_MAP["__total__"]
    cf = np.zeros((128, nf), np.float32)
    cb = np.zeros((128, nb), ml_dtypes.bfloat16)
    for name, entry in CSTF_MAP.items():
        if name == "__total__":
            continue
        off, rows, cols = entry
        cf[0:rows, off:off + cols] = vals[name]
    for name, entry in CSTB_MAP.items():
        if name == "__total__":
            continue
        off, rows, cols = entry
        cb[0:rows, off:off + cols] = vals[name]
    return np.ascontiguousarray(cf), np.ascontiguousarray(cb)


def _prep_percore(core):
    b, p = divmod(core, 4)
    k = {}
    mD = np.zeros((128, 3), np.float32)
    mS = np.zeros((128, 3), np.float32)
    for q in range(3):
        if q < p:
            mS[:, q] = 1.0
            mD[:, q] = 1.0
    k["glaD"] = np.ascontiguousarray(mD)
    k["glaDinv"] = np.ascontiguousarray(1.0 - mD)
    k["glaS"] = np.ascontiguousarray(mS)
    mb = np.zeros((128, 4), np.float32)
    for j in range(4):
        if j > p:
            mb[:, j] = BIGNEG
    k["attn_mb"] = np.ascontiguousarray(mb)
    tril512 = (np.arange(512)[:, None] <= np.arange(512)[None, :])
    am = np.ones((128, 16, 512), ml_dtypes.bfloat16)
    for sblk in range(4):
        am[:, 4 * p + sblk, :] = tril512[128 * sblk:128 * sblk + 128, :].astype(
            ml_dtypes.bfloat16)
    k["amask"] = np.ascontiguousarray(am.reshape(128, 8192))
    k["aidxq"] = (4 * np.arange(128) + p).astype(np.int32).reshape(128, 1)
    k["aidxo"] = (1024 * p + np.arange(128)).astype(np.int32).reshape(128, 1)
    selL = np.zeros((128, 4), np.float32)
    selR = np.zeros((128, 4), np.float32)
    if p > 0:
        selL[:, p - 1] = 1.0
    if p < 3:
        selR[:, p + 1] = 1.0
    k["selL"] = selL
    k["selR"] = selR
    return k



class Ctr:
    def __init__(self):
        self.c = {}

    def add(self, name, delta):
        self.c[name] = self.c.get(name, 0) + delta
        return self.c[name]

    def get(self, name):
        return self.c.get(name, 0)


def build_nc(stage=_STAGE):
    nc = bass.Bass("TRN2", target_bir_lowering=False)
    dt_in = {}

    def din(name, shape, dtype=F32):
        dt_in[name] = nc.dram_tensor(name, list(shape), dtype, kind="ExternalInput").ap()
        return dt_in[name]

    xT = din("xT", (D, TC))
    out = nc.dram_tensor("out", [TC, D], F32, kind="ExternalOutput").ap()

    for name, shape in [
        ("Wq", (1024, 512)), ("Wk", (1024, 512)),
        ("Wv", (1024, 1024)), ("gk_w1", (1024, 16)), ("gk_w2", (16, 512)), ("Wg", (1024, 1024)),
        ("Wo", (1024, 1024)), ("W1", (1024, 2048)),
        ("W2", (2048, 1024)), ("f_w1T", (1024, 4096)), ("f_w2T", (4096, 1024)),
        ("a_wq", (1024, 1024)), ("a_wkd", (1024, 64)),
        ("a_wvd", (1024, 64)), ("a_wku", (32, 128)), ("a_wvu", (32, 128)), ("a_wo", (1024, 1024)),
        ("e_Wc", (1024, 128)), ("e_W1f", (1024, 4096)),
        ("e_W2f", (4096, 1024)), ("e_b2f", (128, 1024)),
        ("cstf_all", (128, CSTF_MAP["__total__"])),
    ]:
        din(name, shape)
    din("cstb_all", (128, CSTB_MAP["__total__"]), BF16)
    din("amask", (128, 8192), BF16)
    din("e_msel", (128, 4096), BF16)
    din("aidxq", (128, 1), I32)
    din("aidxo", (128, 1), I32)

    h_sp = nc.dram_tensor("h_sp", [128, ND, TC], BF16).ap()
    h2_sp = nc.dram_tensor("h2_sp", [128, ND, TC], BF16).ap()
    gla_in = nc.dram_tensor("gla_in", [128, 260], F32)
    gla_out = nc.dram_tensor("gla_out", [512, 260], F32)
    w1q_sp = nc.dram_tensor("w1q_sp", [128, 8, 4096], BF16).ap()
    ag_q_in = nc.dram_tensor("ag_q_in", [4096, 512], BF16)
    ag_q_out = nc.dram_tensor("ag_q_out", [16384, 512], BF16)
    ag_o_in = nc.dram_tensor("ag_o_in", [4096, 512], BF16)
    ag_o_out = nc.dram_tensor("ag_o_out", [16384, 512], BF16)
    ag_k_in = nc.dram_tensor("ag_k_in", [256, 2048], BF16)
    ag_k_out = nc.dram_tensor("ag_k_out", [1024, 2048], BF16)
    ag_vin = nc.dram_tensor("ag_vin", [2048, 256], BF16)
    ag_vout = nc.dram_tensor("ag_vout", [8192, 256], BF16)
    w2q_sp = nc.dram_tensor("w2q_sp", [128, 8, 32, 128], BF16).ap()
    hf_sp = nc.dram_tensor("hf_sp", [128, 32, 2048], F32).ap()
    innT_sp = nc.dram_tensor("innT_sp", [128, 32, 2048], BF16).ap()
    xh_in = nc.dram_tensor("xh_in", [128, 16], F32)
    xh_out = nc.dram_tensor("xh_out", [512, 16], F32)

    ctr = Ctr()

    from contextlib import ExitStack
    es = ExitStack()
    with es:
        x_res = es.enter_context(nc.sbuf_tensor("x_res", [128, ND * TC], F32))
        bufA = es.enter_context(nc.sbuf_tensor("bufA", [128, 8192], F32))
        bufB = es.enter_context(nc.sbuf_tensor("bufB", [128, 8192], F32))
        bufC = es.enter_context(nc.sbuf_tensor("bufC", [128, 4096], F32))
        bufD = es.enter_context(nc.sbuf_tensor("bufD", [128, 2048], F32))
        smll = es.enter_context(nc.sbuf_tensor("smll", [128, 1024], F32))
        wgt = es.enter_context(nc.sbuf_tensor("wgt", [128, 8832], F32))
        cstf = es.enter_context(nc.sbuf_tensor("cstf", [128, CSTF_MAP["__total__"]], F32))
        cstb = es.enter_context(nc.sbuf_tensor("cstb", [128, CSTB_MAP["__total__"]], BF16))
        ps_a = es.enter_context(nc.psum_tensor("ps_a", [128, 2048], F32))
        ps_b = es.enter_context(nc.psum_tensor("ps_b", [128, 1536], F32))
        ps_t = es.enter_context(nc.psum_tensor("ps_t", [128, 1024], BF16))
        dL = es.enter_context(nc.semaphore("dL"))
        dG = es.enter_context(nc.semaphore("dG"))
        dH0 = es.enter_context(nc.semaphore("dH0"))
        dH1 = es.enter_context(nc.semaphore("dH1"))
        dW0 = es.enter_context(nc.semaphore("dW0"))
        dW1 = es.enter_context(nc.semaphore("dW1"))
        dI = es.enter_context(nc.semaphore("dI"))
        dA0 = es.enter_context(nc.semaphore("dA0"))
        dA1 = es.enter_context(nc.semaphore("dA1"))
        tP = es.enter_context(nc.semaphore("tP"))
        vP = es.enter_context(nc.semaphore("vP"))
        sP = es.enter_context(nc.semaphore("sP"))
        gP = es.enter_context(nc.semaphore("gP"))
        block = es.enter_context(nc.Block())

        sems = dict(dL=dL, dG=dG, dH0=dH0, dH1=dH1, dW0=dW0, dW1=dW1, dI=dI,
                    dA0=dA0, dA1=dA1, tP=tP, vP=vP, sP=sP, gP=gP)

        _REC = {"matmul", "transpose", "activation", "copy", "tensor_tensor",
                "tensor_copy", "tensor_scalar", "tensor_reduce", "memset",
                "scalar_tensor_tensor", "dma_start", "indirect_dma_start",
                "collective_compute", "iota", "tensor_relu", "reciprocal",
                "select", "tensor_add", "tensor_mul", "tensor_sub", "tensor_max"}

        class _Rec:
            __slots__ = ("_e", "last")

            def __init__(self, e):
                self._e = e
                self.last = None

            def __getattr__(self, k):
                a = getattr(self._e, k)
                if k in _REC and callable(a):
                    def wrap(*args, **kw):
                        r = a(*args, **kw)
                        self.last = r
                        return r
                    return wrap
                return a

        class _BlockProxy:
            def __init__(self, b):
                self._b = b

            def _mk(self, name):
                orig = getattr(self._b, name)

                def deco(f):
                    def g(eng, *a, **kw):
                        return f(_Rec(eng), *a, **kw)
                    return orig(g)
                return deco

            @property
            def tensor(self):
                return self._mk("tensor")

            @property
            def vector(self):
                return self._mk("vector")

            @property
            def scalar(self):
                return self._mk("scalar")

            @property
            def gpsimd(self):
                return self._mk("gpsimd")

            @property
            def sync(self):
                return self._mk("sync")

        block = _BlockProxy(block)

        def inc(inst, name, delta=1):
            ctr.add(name, delta)
            inst.then_inc(sems[name], delta)

        def semi(eng, name, delta=1):
            ctr.add(name, delta)
            inst = getattr(eng, "last", None)
            if inst is None:
                eng.drain().then_inc(sems[name], delta)
            else:
                inst.then_inc(sems[name], delta)

        def wait(eng, name, val=None):
            eng.wait_ge(sems[name], ctr.get(name) if val is None else val)

        def dseq(sy_eng, out_ap, in_ap, **kw):
            v = ctr.get("dL")
            if v > 0:
                sy_eng.wait_ge(sems["dL"], v)
            ctr.add("dL", 16)
            sy_eng.dma_start(out=out_ap, in_=in_ap, **kw).then_inc(sems["dL"], 16)

        def wait_all(eng, but=()):
            for name, h in sems.items():
                if name in but:
                    continue
                v = ctr.get(name)
                if v > 0:
                    eng.wait_ge(h, v)

        def cst(name):
            if name == "__f_all__":
                return cstf[:, :]
            if name == "__b_all__":
                return cstb[:, :]
            if name in CSTF_MAP:
                off, rows, cols = CSTF_MAP[name]
                return cstf[0:rows, off:off + cols]
            off, rows, cols = CSTB_MAP[name]
            return cstb[0:rows, off:off + cols]

        soff = CSTF_MAP["scals"][0]
        for i, val in enumerate(SCAL_VALS):
            if val != 0.0:
                nc.const_aps.register(F32, float(val), cstf[:, soff + i:soff + i + 1])

        env = dict(nc=nc, block=block, inp=dt_in, out=out, ctr=ctr, inc=inc, wait=wait,
                   semi=semi, sems=sems, wait_all=wait_all, cst=cst, dseq=dseq,
                   x_res=x_res, bufA=bufA, bufB=bufB, bufC=bufC, bufD=bufD, smll=smll,
                   wgt=wgt, ps_a=ps_a, ps_b=ps_b, ps_t=ps_t,
                   h_sp=h_sp, h2_sp=h2_sp, gla_in=gla_in, gla_out=gla_out,
                   w1q_sp=w1q_sp, w2q_sp=w2q_sp, hf_sp=hf_sp, innT_sp=innT_sp,
                   xh_in=xh_in, xh_out=xh_out,
                   ag_q_in=ag_q_in, ag_q_out=ag_q_out, ag_o_in=ag_o_in,
                   ag_o_out=ag_o_out, ag_k_in=ag_k_in, ag_k_out=ag_k_out,
                   ag_vin=ag_vin, ag_vout=ag_vout)

        _phase0_load_x(env)
        if stage >= 1:
            _phase1_l0attn(env, stage)
        if stage >= 3:
            _phase2_l0ffn(env)
        if stage >= 4:
            _phase3_l1attn(env)
        if stage >= 5:
            _phase4_l1ffn(env)
        _phase_out(env)

    return nc



def emit_ln(env, src_fn, dst_fn, rX, rY, sq2, tmp, wb=None, src_bf16=False,
            dst_block_fn=None, post_block=None):
    block = env["block"]
    wait_all = env["wait_all"]; semi = env["semi"]
    ps_a = env["ps_a"]
    onescol = env["cst"]("onescol_b" if src_bf16 else "onescol")

    def src_blk(dd, nb):
        ap = src_fn(dd)
        if ap.ndim == 2:
            return ap[:, 512 * nb:512 * nb + 512]
        return ap[:, 4 * nb:4 * nb + 4, :]

    @block.tensor
    def _(t):
        wait_all(t, but=("dG",))
        for nb in range(4):
            for dd in range(ND):
                t.matmul(ps_a[0:1, 512 * nb:512 * nb + 512], onescol,
                         src_blk(dd, nb), start=(dd == 0), stop=(dd == ND - 1))
        semi(t, "tP")

    @block.vector
    def _(v):
        wait_all(v, but=("dG",))
        v.tensor_copy(rX, ps_a[0:1, 0:2048])
        semi(v, "vP")

    for dd in range(ND):
        @block.scalar
        def _(s, dd=dd):
            wait_all(s, but=("dG",))
            s.activation(sq2[:, 0:2048], src_fn(dd), AF.Square)
            semi(s, "sP")

        @block.tensor
        def _(t, dd=dd):
            wait_all(t, but=("dG",))
            for nb in range(4):
                t.matmul(ps_a[32:33, 512 * nb:512 * nb + 512], env["cst"]("onescol"),
                         sq2[:, 512 * nb:512 * nb + 512],
                         start=(dd == 0), stop=(dd == ND - 1))
            semi(t, "tP")

    @block.vector
    def _(v):
        wait_all(v, but=("dG",))
        v.tensor_copy(rY, ps_a[32:33, 0:2048])
        v.drain()
        v.tensor_scalar(rX, rX, 1.0 / D, None, ALU.mult)
        v.tensor_scalar(rY, rY, 1.0 / D, None, ALU.mult)
        for cb in range(4):
            sl = slice(512 * cb, 512 * cb + 512)
            v.drain()
            v.tensor_tensor(tmp[0:1, :], rX[:, sl], rX[:, sl], ALU.mult)
            v.drain()
            v.tensor_tensor(rY[:, sl], rY[:, sl], tmp[0:1, :], ALU.subtract)
        semi(v, "vP")

    @block.scalar
    def _(s):
        wait_all(s, but=("dG",))
        s.activation(rY, rY, AF.Ln, bias=EPS)
        s.drain()
        s.activation(rY, rY, AF.Exp, scale=-0.5)
        semi(s, "sP")

    @block.vector
    def _(v):
        wait_all(v, but=("dG",))
        v.tensor_tensor(rX, rX, rY, ALU.mult)
        v.drain()
        v.tensor_scalar(rX, rX, -1.0, None, ALU.mult)
        semi(v, "vP")

    for nb in range(4):
        @block.tensor
        def _(t, nb=nb):
            wait_all(t, but=("dG",))
            t.matmul(ps_a[:, 0:512], env["cst"]("ones1"), rY[:, 512 * nb:512 * nb + 512],
                     start=True, stop=True)
            t.matmul(ps_a[:, 512:1024], env["cst"]("ones1"), rX[:, 512 * nb:512 * nb + 512],
                     start=True, stop=True)
            semi(t, "tP")

        @block.vector
        def _(v, nb=nb):
            wait_all(v, but=("dG",))
            for dd in range(ND):
                sblk = src_blk(dd, nb)
                if sblk.ndim == 3:
                    a_rep = ps_a[:, 0:512].rearrange("p (c u) -> p c u", u=128)
                    b_rep = ps_a[:, 512:1024].rearrange("p (c u) -> p c u", u=128)
                else:
                    a_rep = ps_a[:, 0:512]
                    b_rep = ps_a[:, 512:1024]
                tmpv = tmp if sblk.ndim == 2 else tmp.rearrange("p (c u) -> p c u", u=128)
                dblk = (dst_block_fn(dd, nb) if dst_block_fn is not None
                        else dst_fn(dd)[:, 512 * nb:512 * nb + 512])
                if dblk.ndim != sblk.ndim and sblk.ndim == 3:
                    dblk = dblk.rearrange("p (c u) -> p c u", u=128)
                v.drain()
                v.tensor_tensor(tmpv, sblk, a_rep, ALU.mult)
                v.drain()
                if wb is None:
                    v.tensor_tensor(dblk, tmpv, b_rep, ALU.add)
                else:
                    v.tensor_tensor(tmpv, tmpv, b_rep, ALU.add)
                    v.drain()
                    w_ap, b_ap = wb(dd)
                    v.tensor_scalar(dblk, tmp, w_ap, b_ap, ALU.mult, ALU.add)
            semi(v, "vP")

        if post_block is not None:
            post_block(nb)


def _phase0_load_x(env):
    block = env["block"]; inp = env["inp"]; inc = env["inc"]; dseq = env["dseq"]
    x_res = env["x_res"]

    @block.sync
    def _(sy):
        ap = inp["xT"].rearrange("(n p) t -> p n t", p=128)
        dst = x_res[:, 0:ND * TC].rearrange("p (n t) -> p n t", n=ND)
        dseq(sy, dst, ap)
        dseq(sy, env["cst"]("__f_all__"), inp["cstf_all"])
        dseq(sy, env["cst"]("__b_all__"), inp["cstb_all"])


def _wload(env, dst_bf16_ap, src_name):
    src = env["inp"][src_name]
    r, c = src.tensor.shape
    nk = (r + 127) // 128
    if nk == 1:
        ap = src
        dst = dst_bf16_ap
    else:
        ap = src.rearrange("(n p) c -> p n c", p=128)
        dst = dst_bf16_ap.rearrange("p (n c) -> p n c", n=nk)
    return env["block"], ap, dst


def _phase1_l0attn(env, stage):
    cst = env["cst"]; dseq = env["dseq"]
    nc = env["nc"]; block = env["block"]; inp = env["inp"]
    ctr = env["ctr"]; inc = env["inc"]; semi = env["semi"]; wait_all = env["wait_all"]
    x_res = env["x_res"]; bufA = env["bufA"]; bufB = env["bufB"]; bufC = env["bufC"]
    bufD = env["bufD"]; smll = env["smll"]; wgt = env["wgt"]
    ps_a = env["ps_a"]; ps_b = env["ps_b"]; ps_t = env["ps_t"]
    h_sp = env["h_sp"]; h2_sp = env["h2_sp"]
    sems = env["sems"]

    bufA_b = bufA[:, :].bitcast(BF16)
    bufB_b = bufB[:, :].bitcast(BF16)
    bufC_b = bufC[:, :].bitcast(BF16)
    bufD_b = bufD[:, :].bitcast(BF16)
    smll_b = smll[:, :].bitcast(BF16)
    wgt_b = wgt[:, :].bitcast(BF16)

    z1a = bufA_b[:, 0:2048]
    qtc = bufA_b[:, 2048:10240]
    ogT = bufA_b[:, 10240:12288]
    hstage = bufA_b[:, 12288:16384]
    wWq = wgt_b[:, 0:4096]
    wWk = wgt_b[:, 4096:8192]
    wWv = wgt_b[:, 8192:16384]
    wWg = wgt_b[:, 0:8192]
    wWo = wgt_b[:, 8192:16384]
    wgk1 = wgt_b[:, 16384:16512]
    wgk2 = wgt_b[:, 16512:17536]
    S_ = smll[:, 0:256]
    Sbf = smll_b[:, 512:768]
    Dcols = smll[:, 384:448]
    Dprod = smll[:, 448:452]
    rrms = smll[:, 452:468]
    ogbuf = smll_b[:, 936:1960]

    @block.gpsimd
    def _(g):
        for dst, name in [(wWq, "Wq"), (wWk, "Wk"), (wWv, "Wv")]:
            _, ap, d = _wload(env, dst, name)
            inc(g.dma_start(out=d, in_=ap), "dG", 16)
        ap = inp["gk_w1"].rearrange("(n p) c -> p n c", p=128)
        inc(g.dma_start(out=wgk1.rearrange("p (n c) -> p n c", n=8), in_=ap), "dG", 16)
        inc(g.dma_start(out=wgk2[0:16, 0:512], in_=inp["gk_w2"]), "dG", 16)

    rX = bufA[0:1, 1024:3072]
    rY = bufA[0:1, 3072:5120]
    sq2 = bufC[:, 0:2048]
    tmpb = smll[:, 0:512]

    if stage == 1:
        emit_ln(env, lambda dd: x_res[:, dd * TC:(dd + 1) * TC],
                lambda dd: x_res[:, dd * TC:(dd + 1) * TC], rX, rY, sq2, tmpb,
                wb=lambda dd: (cst("ln_w8")[:, dd:dd + 1], cst("ln_b8")[:, dd:dd + 1]))
        return

    def h_dst_block(dd, nb):
        return hstage[:, dd * 512: dd * 512 + 512]

    def h_post_block(nb):
        @block.sync
        def _(sy):
            wait_all(sy)
            dst = h_sp[:, :, 512 * nb:512 * nb + 512]
            dseq(sy, dst, hstage.rearrange("p (n c) -> p n c", n=ND))

        @block.tensor
        def _(t):
            wait_all(t, but=("dL", "dH0", "dH1", "dW0", "dW1"))
            t.wait_ge(sems["dG"], ctr.get("dG"))
            for dd in range(ND):
                t.matmul(ps_b[0:16, 0:512], wgk1[:, dd * 16:dd * 16 + 16],
                         hstage[:, dd * 512:dd * 512 + 512],
                         start=(dd == 0), stop=(dd == ND - 1))
            semi(t, "tP")

        @block.vector
        def _(v):
            wait_all(v, but=("dL", "dH0", "dH1", "dW0", "dW1"))
            v.tensor_copy(z1a[0:16, 512 * nb:512 * nb + 512], ps_b[0:16, 0:512])
            semi(v, "vP")

    emit_ln(env, lambda dd: x_res[:, dd * TC:(dd + 1) * TC], None, rX, rY, sq2, tmpb,
            wb=lambda dd: (cst("ln_w8")[:, dd:dd + 1], cst("ln_b8")[:, dd:dd + 1]),
            dst_block_fn=h_dst_block, post_block=h_post_block)

    if _SUB == 1:
        return
    hst = [bufA_b[:, 12288:13312], bufA_b[:, 13312:14336]]
    hdma = {}

    def h_fetch(sy, c, buf_i):
        dst = hst[buf_i].rearrange("p (n c) -> p n c", n=ND)
        sname = "dH%d" % buf_i
        inc(sy.dma_start(out=dst, in_=h_sp[:, :, 128 * c:128 * c + 128]), sname, 16)
        hdma[c] = (sname, ctr.get(sname))

    @block.vector
    def _(v):
        wait_all(v)
        v.memset(S_, 0.0)
        v.memset(Dprod, 1.0)
        semi(v, "vP")

    @block.sync
    def _(sy):
        wait_all(sy)
        h_fetch(sy, 0, 0)
        h_fetch(sy, 1, 1)

    qbf = bufD_b[:, 0:512]
    kbf = bufD_b[:, 512:1024]
    vbf = bufD_b[:, 1024:2048]
    Abf = bufD_b[:, 2048:4096]
    ktT = bufD_b[:, 0:512]
    spA = bufC[:, 0:512]
    gsA = bufC[:, 512:1024]
    drA = bufC[:, 1024:1536]
    eqA = bufC[:, 1536:2048]

    for c in range(NC_CHUNK := 16):
        hb = hst[c % 2]

        @block.tensor
        def _(t, c=c, hb=hb):
            wait_all(t, but=("dL", "dG", "dH0", "dH1", "dW0", "dW1"))
            t.wait_ge(sems[hdma[c][0]], hdma[c][1])
            for dd in range(ND):
                t.matmul(ps_a[:, 0:512], hb[:, dd * 128:dd * 128 + 128],
                         wWq[:, dd * 512:dd * 512 + 512], start=(dd == 0), stop=(dd == ND - 1))
            for dd in range(ND):
                t.matmul(ps_a[:, 512:1024], hb[:, dd * 128:dd * 128 + 128],
                         wWk[:, dd * 512:dd * 512 + 512], start=(dd == 0), stop=(dd == ND - 1))
            for half in range(2):
                for dd in range(ND):
                    t.matmul(ps_a[:, 1024 + 512 * half:1536 + 512 * half],
                             hb[:, dd * 128:dd * 128 + 128],
                             wWv[:, dd * 1024 + 512 * half:dd * 1024 + 512 * half + 512],
                             start=(dd == 0), stop=(dd == ND - 1))
            t.matmul(ps_b[:, 0:512], z1a[0:16, 128 * c:128 * c + 128],
                     wgk2[0:16, 0:512], start=True, stop=False)
            t.matmul(ps_b[:, 0:512], cst("ones1"),
                     cst("gkb2"), start=False, stop=True)
            semi(t, "tP")

        if c + 2 < 16:
            @block.sync
            def _(sy, c=c):
                sy.wait_ge(sems["tP"], ctr.get("tP"))
                h_fetch(sy, c + 2, c % 2)

        if _CP < 2:
            continue
        @block.scalar
        def _(s):
            wait_all(s, but=("dL", "dG", "dH0", "dH1", "dW0", "dW1"))
            s.activation(gsA, ps_b[:, 0:512], AF.Relu, scale=-1.0)
            s.activation(spA, ps_b[:, 0:512], AF.Abs)
            s.drain()
            s.activation(spA, spA, AF.Exp, scale=-1.0)
            s.drain()
            s.activation(spA, spA, AF.Ln, bias=1.0)
            semi(s, "sP")

        @block.vector
        def _(v):
            wait_all(v, but=("dL", "dG", "dH0", "dH1", "dW0", "dW1"))
            v.tensor_tensor(spA, spA, gsA, ALU.add)
            v.tensor_copy(Sbf, S_)
            semi(v, "vP")

        @block.tensor
        def _(t):
            wait_all(t, but=("dL", "dG", "dH0", "dH1", "dW0", "dW1"))
            t.matmul(ps_b[:, 512:1024], cst("triu16"), spA, start=True, stop=True)
            t.matmul(ps_b[0:1, 1024:1536], cst("onescol"), spA, start=True, stop=True)
            semi(t, "tP")

        @block.vector
        def _(v):
            wait_all(v, but=("dL", "dG", "dH0", "dH1", "dW0", "dW1"))
            v.tensor_copy(gsA, ps_b[:, 512:1024])
            semi(v, "vP")

        @block.scalar
        def _(s):
            wait_all(s, but=("dL", "dG", "dH0", "dH1", "dW0", "dW1"))
            s.activation(drA[0:1, :], ps_b[0:1, 1024:1536], AF.Exp, scale=-1.0 / 16.0)
            s.activation(eqA, ps_b[:, 512:1024], AF.Exp, bias=LNSCALE)
            s.drain()
            s.activation(gsA, ps_b[:, 512:1024], AF.Exp, scale=-1.0)
            semi(s, "sP")

        if _CP < 3:
            continue
        @block.tensor
        def _(t):
            wait_all(t, but=("dL", "dG", "dH0", "dH1", "dW0", "dW1"))
            for g in range(4):
                t.transpose(ps_b[:, 1024 + g:1024 + g + 1], drA[0:1, 128 * g:128 * g + 128],
                            cst("identf")[0:1, 0:1])
            semi(t, "tP")

        @block.vector
        def _(v, c=c):
            wait_all(v, but=("dL", "dG", "dH0", "dH1", "dW0", "dW1"))
            v.tensor_copy(Dcols[:, 4 * c:4 * c + 4], ps_b[:, 1024:1028])
            v.tensor_tensor(qbf, ps_a[:, 0:512], eqA, ALU.mult)
            v.tensor_tensor(kbf, ps_a[:, 512:1024], gsA, ALU.mult)
            v.tensor_copy(vbf, ps_a[:, 1024:2048])
            semi(v, "vP")

        if _CP < 4:
            continue
        @block.tensor
        def _(t):
            wait_all(t, but=("dL", "dG", "dH0", "dH1", "dW0", "dW1"))
            for g in range(4):
                t.transpose(ps_t[:, 128 * g:128 * g + 128], qbf[:, 128 * g:128 * g + 128],
                            cst("identb"))
            for g in range(4):
                t.transpose(ps_t[:, 512 + 128 * g:640 + 128 * g], kbf[:, 128 * g:128 * g + 128],
                            cst("identb"))
            semi(t, "tP")

        @block.scalar
        def _(s, c=c):
            wait_all(s, but=("dL", "dG", "dH0", "dH1", "dW0", "dW1"))
            s.copy(qtc[:, 512 * c:512 * c + 512], ps_t[:, 0:512])
            semi(s, "sP")

        @block.vector
        def _(v):
            wait_all(v, but=("dL", "dG", "dH0", "dH1", "dW0", "dW1"))
            v.tensor_copy(ktT, ps_t[:, 512:1024])
            semi(v, "vP")

        if _CP < 5:
            continue
        bankmap = [ps_b[:, 0:512], ps_b[:, 512:1024], ps_b[:, 1024:1536], ps_a[:, 0:512]]

        @block.tensor
        def _(t, c=c):
            wait_all(t, but=("dL", "dG", "dH0", "dH1", "dW0", "dW1"))
            for h in range(16):
                g, a = divmod(h, 4)
                t.matmul(bankmap[g][:, 128 * a:128 * a + 128],
                         ktT[32 * a:32 * a + 32, 128 * g:128 * g + 128],
                         qtc[32 * a:32 * a + 32, (4 * c + g) * 128:(4 * c + g) * 128 + 128],
                         start=True, stop=True, tile_position=(32 * a, 0))
                t.drain()
            semi(t, "tP")

        @block.vector
        def _(v):
            wait_all(v, but=("dL", "dG", "dH0", "dH1", "dW0", "dW1"))
            for g in range(4):
                v.tensor_tensor(Abf[:, 512 * g:512 * g + 512].rearrange("p (a c) -> p a c", a=4),
                                bankmap[g].rearrange("p (a c) -> p a c", a=4),
                                cst("trilf").unsqueeze(1).broadcast_to([128, 4, 128]),
                                ALU.mult)
            semi(v, "vP")

        if _CP < 6:
            continue
        @block.tensor
        def _(t, c=c):
            wait_all(t, but=("dL", "dG", "dH0", "dH1", "dW0", "dW1"))
            for h in range(16):
                g, a = divmod(h, 4)
                t.matmul(ps_a[:, 1024 + 64 * h:1088 + 64 * h],
                         Abf[:, 512 * g + 128 * a:512 * g + 128 * a + 128],
                         vbf[:, 64 * h:64 * h + 64], start=True, stop=False)
                t.drain()
                t.matmul(ps_a[:, 1024 + 64 * h:1088 + 64 * h],
                         qtc[32 * a:32 * a + 32, (4 * c + g) * 128:(4 * c + g) * 128 + 128],
                         Sbf[32 * a:32 * a + 32, 64 * g:64 * g + 64],
                         start=False, stop=True, tile_position=(32 * a, 0))
                t.drain()
            for h in range(16):
                g, a = divmod(h, 4)
                t.matmul(ps_b[32 * a:32 * a + 32, 64 * g:64 * g + 64],
                         kbf[:, 32 * h:32 * h + 32], vbf[:, 64 * h:64 * h + 64],
                         start=True, stop=True, tile_position=(0, 32 * a))
                t.drain()
            semi(t, "tP")

        @block.vector
        def _(v, c=c):
            wait_all(v, but=("dL", "dG", "dH0", "dH1", "dW0", "dW1"))
            v.tensor_copy(bufB_b[:, 1024 * c:1024 * c + 1024], ps_a[:, 1024:2048])
            for g in range(4):
                v.tensor_tensor(S_[:, 64 * g:64 * g + 64], S_[:, 64 * g:64 * g + 64],
                                ps_b[:, 64 * g:64 * g + 64], ALU.add)
            v.drain()
            for g in range(4):
                v.tensor_scalar(S_[:, 64 * g:64 * g + 64], S_[:, 64 * g:64 * g + 64],
                                Dcols[:, 4 * c + g:4 * c + g + 1], None, ALU.mult)
            v.drain()
            v.tensor_tensor(Dprod, Dprod, Dcols[:, 4 * c:4 * c + 4], ALU.mult)
            semi(v, "vP")


    if _SUB == 2:
        return
    @block.sync
    def _(sy):
        wait_all(sy, but=("dG",))
        dseq(sy, env["gla_in"].ap()[:, 0:4], Dprod)
        dseq(sy, env["gla_in"].ap()[:, 4:260], S_)

    @block.gpsimd
    def _(g):
        wait_all(g)
        inc(g.collective_compute(
            "AllGather", ALU.bypass,
            replica_groups=[[0, 1, 2, 3], [4, 5, 6, 7]],
            ins=[env["gla_in"].ap().opt()], outs=[env["gla_out"].ap().opt()],
        ), "gP", 1)
        _, ap, d = _wload(env, wWg, "Wg")
        inc(g.dma_start(out=d, in_=ap), "dG", 16)
        _, ap, d = _wload(env, wWo, "Wo")
        inc(g.dma_start(out=d, in_=ap), "dG", 16)

    @block.vector
    def _(v):
        wait_all(v, but=("dG",))
        v.memset(S_, 0.0)
        semi(v, "vP")

    for p in range(3):
        @block.sync
        def _(sy, p=p):
            wait_all(sy, but=("dG",))
            dseq(sy, bufC[:, 0:4], env["gla_out"].ap()[128 * p:128 * p + 128, 0:4])
            dseq(sy, bufC[:, 4:260], env["gla_out"].ap()[128 * p:128 * p + 128, 4:260])

        @block.vector
        def _(v, p=p):
            wait_all(v, but=("dG",))
            v.tensor_scalar(bufC[:, 260:264], bufC[:, 0:4], cst("glaD")[:, p:p + 1],
                            None, ALU.mult)
            v.drain()
            v.tensor_tensor(bufC[:, 260:264], bufC[:, 260:264],
                            cst("glaDinv")[:, p:p + 1].broadcast_to([128, 4]), ALU.add)
            v.drain()
            v.tensor_tensor(S_.rearrange("p (a b) -> p a b", a=4),
                            S_.rearrange("p (a b) -> p a b", a=4),
                            bufC[:, 260:264].unsqueeze(2).broadcast_to([128, 4, 64]),
                            ALU.mult)
            v.drain()
            v.scalar_tensor_tensor(S_, bufC[:, 4:260], cst("glaS")[:, p:p + 1], S_,
                                   ALU.mult, ALU.add)
            semi(v, "vP")

    if _SUB == 3:
        return
    xmid_chunk = lambda c: bufB_b[:, 1024 * c:1024 * c + 1024]

    @block.sync
    def _(sy):
        wait_all(sy, but=("dG",))
        h_fetch(sy, 0, 0)
        h_fetch(sy, 1, 1)

    for c in range(16):
        hb = hst[c % 2]

        @block.vector
        def _(v):
            wait_all(v, but=("dL", "dG", "dH0", "dH1", "dW0", "dW1"))
            v.tensor_copy(Sbf, S_)
            semi(v, "vP")

        @block.tensor
        def _(t, c=c, hb=hb):
            wait_all(t, but=("dL", "dG", "dH0", "dH1", "dW0", "dW1"))
            t.wait_ge(sems[hdma[c][0]], hdma[c][1])
            t.wait_ge(sems["dG"], ctr.get("dG"))
            for h in range(16):
                g, a = divmod(h, 4)
                t.matmul(ps_a[:, 64 * h:64 * h + 64],
                         qtc[32 * a:32 * a + 32, (4 * c + g) * 128:(4 * c + g) * 128 + 128],
                         Sbf[32 * a:32 * a + 32, 64 * g:64 * g + 64],
                         start=True, stop=True, tile_position=(32 * a, 0))
                t.drain()
            for half in range(2):
                for dd in range(ND):
                    t.matmul(ps_b[:, 512 * half:512 * half + 512],
                             hb[:, dd * 128:dd * 128 + 128],
                             wWg[:, dd * 1024 + 512 * half:dd * 1024 + 512 * half + 512],
                             start=(dd == 0), stop=(dd == ND - 1))
            semi(t, "tP")

        @block.vector
        def _(v, c=c):
            wait_all(v, but=("dL", "dG", "dH0", "dH1", "dW0", "dW1"))
            v.tensor_tensor(bufD[:, 0:1024], ps_a[:, 0:1024],
                            bufB_b[:, 1024 * c:1024 * c + 1024], ALU.add)
            for g in range(4):
                v.tensor_scalar(S_[:, 64 * g:64 * g + 64], S_[:, 64 * g:64 * g + 64],
                                Dcols[:, 4 * c + g:4 * c + g + 1], None, ALU.mult)
            semi(v, "vP")

        @block.scalar
        def _(s):
            wait_all(s, but=("dL", "dG", "dH0", "dH1", "dW0", "dW1"))
            s.activation(bufD[:, 1024:2048], bufD[:, 0:1024], AF.Square)
            semi(s, "sP")

        @block.vector
        def _(v):
            wait_all(v, but=("dL", "dG", "dH0", "dH1", "dW0", "dW1"))
            v.tensor_reduce(rrms, bufD[:, 1024:2048].rearrange("p (h d) -> p h d", h=16),
                            AX.X, ALU.add)
            semi(v, "vP")

        @block.scalar
        def _(s):
            wait_all(s, but=("dL", "dG", "dH0", "dH1", "dW0", "dW1"))
            s.activation(rrms, rrms, AF.Ln, scale=1.0 / 64.0, bias=EPS)
            s.drain()
            s.activation(rrms, rrms, AF.Exp, scale=-0.5)
            s.activation(bufD[:, 1024:2048], ps_b[:, 0:1024], AF.Silu)
            semi(s, "sP")

        @block.vector
        def _(v):
            wait_all(v, but=("dL", "dG", "dH0", "dH1", "dW0", "dW1"))
            v.tensor_tensor(bufD[:, 1024:2048].rearrange("p (h d) -> p h d", h=16),
                            bufD[:, 1024:2048].rearrange("p (h d) -> p h d", h=16),
                            cst("rmsw_tp").unsqueeze(1).broadcast_to([128, 16, 64]),
                            ALU.mult)
            v.tensor_tensor(bufD[:, 0:1024].rearrange("p (h d) -> p h d", h=16),
                            bufD[:, 0:1024].rearrange("p (h d) -> p h d", h=16),
                            rrms.unsqueeze(2).broadcast_to([128, 16, 64]),
                            ALU.mult)
            v.drain()
            v.tensor_tensor(ogbuf, bufD[:, 0:1024], bufD[:, 1024:2048], ALU.mult)
            semi(v, "vP")

        @block.tensor
        def _(t):
            wait_all(t, but=("dL", "dG", "dH0", "dH1", "dW0", "dW1"))
            for g in range(8):
                t.transpose(ps_t[:, 128 * g:128 * g + 128], ogbuf[:, 128 * g:128 * g + 128],
                            cst("identb"))
            semi(t, "tP")

        @block.scalar
        def _(s):
            wait_all(s, but=("dL", "dG", "dH0", "dH1", "dW0", "dW1"))
            s.copy(ogT[:, 0:1024], ps_t[:, 0:1024])
            semi(s, "sP")

        @block.tensor
        def _(t, c=c):
            wait_all(t, but=("dL", "dG", "dH0", "dH1", "dW0", "dW1"))
            for dd in range(ND):
                for k in range(ND):
                    t.matmul(ps_a[:, 1024 + 128 * (dd % 8):1152 + 128 * (dd % 8)],
                             wWo[:, 1024 * k + 128 * dd:1024 * k + 128 * dd + 128],
                             ogT[:, 128 * k:128 * k + 128],
                             start=(k == 0), stop=(k == ND - 1))
            semi(t, "tP")

        @block.vector
        def _(v, c=c):
            wait_all(v, but=("dL", "dG", "dH0", "dH1", "dW0", "dW1"))
            for dd in range(ND):
                v.tensor_tensor(xmid_chunk(c)[:, 128 * dd:128 * dd + 128],
                                x_res[:, dd * TC + 128 * c:dd * TC + 128 * c + 128],
                                ps_a[:, 1024 + 128 * dd:1152 + 128 * dd], ALU.add)
            semi(v, "vP")

        if c + 2 < 16:
            @block.sync
            def _(sy, c=c):
                sy.wait_ge(sems["tP"], ctr.get("tP"))
                h_fetch(sy, c + 2, c % 2)

    if _SUB == 4:
        return
    _phase1_ffn(env)


def _phase1_ffn(env):
    cst = env["cst"]; dseq = env["dseq"]
    nc = env["nc"]; block = env["block"]; inp = env["inp"]
    ctr = env["ctr"]; inc = env["inc"]; semi = env["semi"]; wait_all = env["wait_all"]
    x_res = env["x_res"]; bufA = env["bufA"]; bufB = env["bufB"]; bufC = env["bufC"]
    bufD = env["bufD"]; smll = env["smll"]; wgt = env["wgt"]
    ps_a = env["ps_a"]; ps_b = env["ps_b"]
    h2_sp = env["h2_sp"]; sems = env["sems"]

    bufA_b = bufA[:, :].bitcast(BF16)
    bufB_b = bufB[:, :].bitcast(BF16)
    bufC_b = bufC[:, :].bitcast(BF16)
    bufD_b = bufD[:, :].bitcast(BF16)

    def xmid_tile(dd):
        return bufB_b[:, 128 * dd:128 * dd + 15 * 1024 + 128].rearrange(
            "p (c u) -> p c u", u=1024)[:, :, 0:128] if False else \
            bufB_b[:, 0:16384].rearrange("p (c u) -> p c u", u=1024)[:, :, 128 * dd:128 * dd + 128]

    rX = bufA[0:1, 0:2048]
    rY = bufA[0:1, 2048:4096]
    sq2 = bufC[:, 0:2048]
    tmpb = smll[:, 0:512]

    def h2_dst_block(dd, nb):
        return bufD_b[:, dd * 512:dd * 512 + 512]

    def h2_post_block(nb):
        @block.sync
        def _(sy):
            wait_all(sy, but=("dG",))
            dseq(sy, h2_sp[:, :, 512 * nb:512 * nb + 512],
                 bufD_b[:, 0:4096].rearrange("p (n c) -> p n c", n=ND))

    emit_ln(env, xmid_tile, None, rX, rY, sq2, tmpb,
            wb=lambda dd: (cst("ln_w8")[:, dd:dd + 1], cst("ln_b8")[:, dd:dd + 1]),
            dst_block_fn=h2_dst_block, post_block=h2_post_block, src_bf16=True)

    @block.gpsimd
    def _(g):
        wait_all(g, but=("dL", "dH0", "dH1", "dW0", "dW1"))
        _, ap, d = _wload(env, bufA_b[:, 0:16384], "W1")
        inc(g.dma_start(out=d, in_=ap), "dG", 16)

    for dd in range(ND):
        @block.vector
        def _(v, dd=dd):
            wait_all(v, but=("dL", "dG", "dH0", "dH1", "dW0", "dW1"))
            v.tensor_tensor(x_res[:, dd * TC:(dd + 1) * TC].rearrange("p (c u) -> p c u", u=128),
                            x_res[:, dd * TC:(dd + 1) * TC].rearrange("p (c u) -> p c u", u=128),
                            xmid_tile(dd), ALU.add)
            semi(v, "vP")

    @block.gpsimd
    def _(g):
        wait_all(g, but=("dL", "dH0", "dH1", "dW0", "dW1"))
        _, ap, d = _wload(env, bufB_b[:, 0:16384], "W2")
        inc(g.dma_start(out=d, in_=ap), "dG", 16)

    for q in range(4):
        @block.sync
        def _(sy, q=q):
            wait_all(sy, but=("dG",))
            dseq(sy, bufD_b[:, 0:4096].rearrange("p (n c) -> p n c", n=ND),
                 h2_sp[:, :, 512 * q:512 * q + 512])

        for ft in range(16):
            @block.tensor
            def _(t, q=q, ft=ft):
                wait_all(t, but=("dG",)) if ft == 0 else None
                if ft == 0:
                    t.wait_ge(sems["dG"], ctr.get("dG"))
                    t.wait_ge(sems["dL"], ctr.get("dL"))
                if ft >= 2:
                    t.wait_ge(sems["sP"], ctr.get("sP") - 1)
                for k in range(ND):
                    t.matmul(ps_a[:, 512 * (ft % 2):512 * (ft % 2) + 512],
                             bufA_b[:, 2048 * k + 128 * ft:2048 * k + 128 * ft + 128],
                             bufD_b[:, 512 * k:512 * k + 512],
                             start=(k == 0), stop=(k == ND - 1))
                semi(t, "tP")

            @block.scalar
            def _(s, q=q, ft=ft):
                s.wait_ge(sems["tP"], ctr.get("tP"))
                s.activation(bufC_b[:, 512 * ft:512 * ft + 512],
                             ps_a[:, 512 * (ft % 2):512 * (ft % 2) + 512],
                             AF.Gelu, bias=cst("b1t")[:, ft:ft + 1])
                semi(s, "sP")

        for dd in range(ND):
            @block.tensor
            def _(t, q=q, dd=dd):
                if dd == 0:
                    wait_all(t, but=("dL", "dG", "dH0", "dH1", "dW0", "dW1"))
                if dd >= 2:
                    t.wait_ge(sems["vP"], ctr.get("vP") - 1)
                for k2 in range(16):
                    t.matmul(ps_a[:, 1024 + 512 * (dd % 2):1536 + 512 * (dd % 2)],
                             bufB_b[:, 1024 * k2 + 128 * dd:1024 * k2 + 128 * dd + 128],
                             bufC_b[:, 512 * k2:512 * k2 + 512],
                             start=(k2 == 0), stop=(k2 == 15))
                semi(t, "tP")

            @block.vector
            def _(v, q=q, dd=dd):
                v.wait_ge(sems["tP"], ctr.get("tP"))
                sl = slice(dd * TC + 512 * q, dd * TC + 512 * q + 512)
                v.scalar_tensor_tensor(x_res[:, sl], ps_a[:, 1024 + 512 * (dd % 2):1536 + 512 * (dd % 2)],
                                       cst("b2t")[:, dd:dd + 1], x_res[:, sl], ALU.add, ALU.add)
                semi(v, "vP")

    emit_ln(env, lambda dd: x_res[:, dd * TC:(dd + 1) * TC],
            lambda dd: x_res[:, dd * TC:(dd + 1) * TC],
            bufD[0:1, 0:2048], bufC[0:1, 2048:4096], bufC[:, 0:2048], smll[:, 0:512])



def _phase2_l0ffn(env):
    cst = env["cst"]; dseq = env["dseq"]
    nc = env["nc"]; block = env["block"]; inp = env["inp"]
    ctr = env["ctr"]; inc = env["inc"]; semi = env["semi"]; wait_all = env["wait_all"]
    x_res = env["x_res"]; bufA = env["bufA"]; bufB = env["bufB"]; bufC = env["bufC"]
    bufD = env["bufD"]; smll = env["smll"]; wgt = env["wgt"]
    ps_a = env["ps_a"]; ps_b = env["ps_b"]
    sems = env["sems"]
    w1q_sp = env["w1q_sp"]; w2q_sp = env["w2q_sp"]; hf_sp = env["hf_sp"]

    bufA_b = bufA[:, :].bitcast(BF16)
    bufB_b = bufB[:, :].bitcast(BF16)
    NTOT1 = 4096.0 * 1024.0

    wstream = [wgt[:, 0:4096], wgt[:, 4096:8192]]
    y1buf = bufA[:, 0:2052]
    htile = bufA[:, 2052:4100]
    w1qct = [bufA_b[:, 8200:9224], bufA_b[:, 9224:10248]]
    i8t = bufA[:, 5200:6224].bitcast(I8)
    xq = bufB_b[:, 0:16384]
    hq = bufB_b[:, 0:16384]
    runmax = bufC[:, 0:2048]
    fxt = bufC[:, 2048:4096]
    sxt = bufD[:, 0:2048]
    runmin = wgt[:, 0:2048]
    cells = smll[:, 0:64]
    mw1rep = smll[:, 64:65]
    mw2rep = smll[:, 65:66]
    haloxq = smll[:, 68:76].bitcast(BF16)[:, 0:16]
    halofx = smll[:, 80:82]
    halos = smll[:, 82:146]
    hmx = smll[:, 146:148]

    @block.sync
    def _(sy):
        wait_all(sy, but=("dG",))
        src_ap = x_res[:, 0:ND * TC].rearrange("p (n t) -> p n t", n=ND)
        dseq(sy, env["xh_in"].ap()[:, 0:8],
             src_ap[:, :, 0:1].rearrange("p n t -> p (n t)"),
             allow_slow_non_contiguous=True)
        dseq(sy, env["xh_in"].ap()[:, 8:16],
             src_ap[:, :, TC - 1:TC].rearrange("p n t -> p (n t)"),
             allow_slow_non_contiguous=True)

    @block.gpsimd
    def _(g):
        wait_all(g)
        inc(g.collective_compute(
            "AllGather", ALU.bypass,
            replica_groups=[[0, 1, 2, 3], [4, 5, 6, 7]],
            ins=[env["xh_in"].ap().opt()], outs=[env["xh_out"].ap().opt()],
        ), "gP", 1)

    @block.sync
    def _(sy):
        wait_all(sy, but=("dG",))
        for q in range(4):
            dseq(sy, halos[:, 16 * q:16 * q + 16],
                 env["xh_out"].ap()[128 * q:128 * q + 128, :])

    @block.vector
    def _(v):
        wait_all(v, but=("dG",))
        hl = bufD[:, 2040:2048]
        v.memset(hl, 0.0)
        for q in range(4):
            v.drain()
            v.scalar_tensor_tensor(hl, halos[:, 16 * q + 8:16 * q + 16],
                                   cst("selL")[:, q:q + 1], hl, ALU.mult, ALU.add)
        v.drain()
        v.tensor_copy(halos[:, 0:8], hl)
        v.drain()
        v.memset(hl, 0.0)
        for q in range(4):
            v.drain()
            v.scalar_tensor_tensor(hl, halos[:, 16 * q:16 * q + 8],
                                   cst("selR")[:, q:q + 1], hl, ALU.mult, ALU.add)
        v.drain()
        v.tensor_copy(halos[:, 8:16], hl)
        semi(v, "vP")

    def wstats2(name_T, ktiles, width, cellcol):
        for i in range(ktiles):
            @block.sync
            def _(sy, i=i):
                if i >= 2:
                    sy.wait_ge(sems["tP"], ctr.get("tP") - 1)
                else:
                    wait_all(sy, but=("dG",))
                ap = inp[name_T].rearrange("(n p) c -> p n c", p=128)[:, i, :]
                dseq(sy, wstream[i % 2][:, 0:width], ap)

            @block.scalar
            def _(s, i=i):
                wait_all(s, but=("dG",))
                s.activation(bufB[:, 0:width], wstream[i % 2][:, 0:width], AF.Relu)
                semi(s, "sP")

            @block.tensor
            def _(t, i=i, cellcol=cellcol):
                wait_all(t, but=("dG",))
                for nb in range(width // 512):
                    t.matmul(ps_b[0:1, 0:512], cst("onescol"),
                             bufB[:, 512 * nb:512 * nb + 512],
                             start=(i == 0 and nb == 0), stop=(i == ktiles - 1 and nb == width // 512 - 1))
                for nb in range(width // 512):
                    t.matmul(ps_b[32:33, 0:512], cst("onescol"),
                             wstream[i % 2][:, 512 * nb:512 * nb + 512],
                             start=(i == 0 and nb == 0), stop=(i == ktiles - 1 and nb == width // 512 - 1))
                semi(t, "tP")

        @block.vector
        def _(v, cellcol=cellcol):
            wait_all(v, but=("dG",))
            v.tensor_reduce(cells[0:1, cellcol:cellcol + 1], ps_b[0:1, 0:512], AX.X, ALU.add)
            v.tensor_reduce(cells[0:1, cellcol + 16:cellcol + 17], ps_b[32:33, 0:512], AX.X, ALU.add)
            v.drain()
            v.tensor_scalar(cells[0:1, cellcol:cellcol + 1], cells[0:1, cellcol:cellcol + 1],
                            2.0, None, ALU.mult)
            v.drain()
            v.tensor_tensor(cells[0:1, cellcol:cellcol + 1], cells[0:1, cellcol:cellcol + 1],
                            cells[0:1, cellcol + 16:cellcol + 17], ALU.subtract)
            v.drain()
            v.tensor_scalar(cells[0:1, cellcol:cellcol + 1], cells[0:1, cellcol:cellcol + 1],
                            1.0 / NTOT1, None, ALU.mult)
            v.drain()
            v.tensor_scalar(cells[0:1, cellcol:cellcol + 1], cells[0:1, cellcol:cellcol + 1],
                            1e-5, None, ALU.max)
            v.drain()
            v.reciprocal(cells[0:1, cellcol + 8:cellcol + 9], cells[0:1, cellcol:cellcol + 1])
            semi(v, "vP")

    wstats2("f_w1T", 8, 4096, 0)
    wstats2("f_w2T", 32, 1024, 1)

    @block.tensor
    def _(t):
        wait_all(t, but=("dG",))
        t.matmul(ps_b[:, 0:1], cst("ones1"), cells[0:1, 0:1], start=True, stop=True)
        t.matmul(ps_b[:, 1:2], cst("ones1"), cells[0:1, 1:2], start=True, stop=True)
        t.matmul(ps_b[:, 2:3], cst("ones1"), cells[0:1, 8:9], start=True, stop=True)
        t.matmul(ps_b[:, 3:4], cst("ones1"), cells[0:1, 9:10], start=True, stop=True)
        semi(t, "tP")

    @block.vector
    def _(v):
        wait_all(v, but=("dG",))
        v.tensor_copy(mw1rep, ps_b[:, 0:1])
        v.tensor_copy(mw2rep, ps_b[:, 1:2])
        v.tensor_copy(smll[:, 66:67], ps_b[:, 2:3])
        v.tensor_copy(smll[:, 67:68], ps_b[:, 3:4])
        semi(v, "vP")

    def wquant(name_T, ktiles, width, s_rep, dst_fn):
        for i in range(ktiles):
            @block.sync
            def _(sy, i=i):
                if i >= 2:
                    sy.wait_ge(sems["sP"], ctr.get("sP") - 1)
                else:
                    wait_all(sy, but=("dG",))
                ap = inp[name_T].rearrange("(n p) c -> p n c", p=128)[:, i, :]
                dseq(sy, wstream[i % 2][:, 0:width], ap)

            @block.vector
            def _(v, i=i):
                wait_all(v, but=("dG",))
                v.tensor_scalar(bufB[:, 4096:4096 + width], wstream[i % 2][:, 0:width],
                                s_rep, MAGIC, ALU.mult, ALU.add)
                v.drain()
                v.tensor_scalar(bufB[:, 4096:4096 + width], bufB[:, 4096:4096 + width],
                                -MAGIC, None, ALU.add)
                v.drain()
                v.tensor_scalar(bufB[:, 0:width // 2].bitcast(BF16)[:, 0:width],
                                bufB[:, 4096:4096 + width], -1.0, 1.0, ALU.max, ALU.min)
                semi(v, "vP")

            @block.sync
            def _(sy, i=i):
                wait_all(sy, but=("dG",))
                dseq(sy, dst_fn(i), bufB[:, 0:width // 2].bitcast(BF16)[:, 0:width])

    wquant("f_w1T", 8, 4096, smll[:, 66:67], lambda i: w1q_sp[:, i, :])
    wquant("f_w2T", 32, 1024, smll[:, 67:68],
           lambda i: w2q_sp[:, :, i, :])

    @block.vector
    def _(v):
        wait_all(v, but=("dG",))
        v.tensor_copy(runmax, x_res[:, 0:TC])
        v.tensor_copy(runmin, x_res[:, 0:TC])
        for dd in range(1, ND):
            v.drain()
            v.tensor_tensor(runmax, runmax, x_res[:, dd * TC:(dd + 1) * TC], ALU.max)
            v.drain()
            v.tensor_tensor(runmin, runmin, x_res[:, dd * TC:(dd + 1) * TC], ALU.min)
        v.drain()
        v.tensor_scalar(runmin, runmin, -1.0, None, ALU.mult)
        v.drain()
        v.tensor_tensor(runmax, runmax, runmin, ALU.max)
        v.drain()
        v.tensor_reduce(hmx, halos[:, 0:16].rearrange("p (s n) -> p s n", s=2),
                        AX.X, ALU.max, apply_absolute_value=True)
        semi(v, "vP")

    @block.gpsimd
    def _(g):
        wait_all(g)
        g.tensor_reduce(runmax[0:1, :], runmax, AX.C, ALU.max)
        g.drain()
        g.tensor_reduce(hmx[0:1, :], hmx, AX.C, ALU.max)
        g.drain().then_inc(sems["gP"], 1)
    env["ctr"].add("gP", 1)

    @block.tensor
    def _(t):
        wait_all(t, but=("dG",))
        for nb in range(4):
            t.matmul(ps_a[:, 512 * nb:512 * nb + 512], cst("ones1"),
                     runmax[0:1, 512 * nb:512 * nb + 512], start=True, stop=True)
        t.matmul(ps_b[:, 0:2], cst("ones1"), hmx[0:1, 0:2], start=True, stop=True)
        semi(t, "tP")

    @block.vector
    def _(v):
        wait_all(v, but=("dG",))
        v.tensor_scalar(runmax, ps_a[:, 0:2048], 1e-5, None, ALU.max)
        v.tensor_scalar(smll[:, 148:150], ps_b[:, 0:2], 1e-5, None, ALU.max)
        v.drain()
        v.reciprocal(sxt, runmax)
        v.drain()
        v.tensor_scalar(sxt, sxt, 127.0, None, ALU.mult)
        v.drain()
        v.reciprocal(smll[:, 150:152], smll[:, 148:150])
        v.drain()
        v.tensor_scalar(smll[:, 150:152], smll[:, 150:152], 127.0, None, ALU.mult)
        v.drain()
        v.scalar_tensor_tensor(fxt, runmax, 1.0 / 127.0, mw1rep.broadcast_to([128, 2048]),
                               ALU.mult, ALU.mult)
        v.drain()
        v.scalar_tensor_tensor(halofx, smll[:, 148:150], 1.0 / 127.0,
                               mw1rep.broadcast_to([128, 2]), ALU.mult, ALU.mult)
        semi(v, "vP")

    for dd in range(ND):
        @block.vector
        def _(v, dd=dd):
            wait_all(v, but=("dG",)) if dd == 0 else None
            v.drain()
            v.tensor_tensor(bufA[:, 4100:6148], x_res[:, dd * TC:(dd + 1) * TC], sxt, ALU.mult)
            v.drain()
            v.tensor_scalar(bufA[:, 4100:6148], bufA[:, 4100:6148], MAGIC, -MAGIC,
                            ALU.add, ALU.add)
            v.drain()
            v.tensor_copy(xq[:, dd * 2048:(dd + 1) * 2048], bufA[:, 4100:6148])
            semi(v, "vP")

    @block.vector
    def _(v):
        wait_all(v, but=("dG",))
        v.tensor_tensor(bufA[:, 4100:4116].rearrange("p (s n) -> p s n", s=2),
                        halos[:, 0:16].rearrange("p (s n) -> p s n", s=2),
                        smll[:, 150:152].unsqueeze(2).broadcast_to([128, 2, 8]), ALU.mult)
        v.drain()
        v.tensor_scalar(haloxq, bufA[:, 4100:4116], MAGIC, -MAGIC, ALU.add, ALU.add)
        semi(v, "vP")

    w1dma = {}

    @block.sync
    def _(sy):
        wait_all(sy, but=("dG",))
        for ci in range(2):
            inc(sy.dma_start(out=w1qct[ci].rearrange("p (n c) -> p n c", n=ND),
                             in_=w1q_sp[:, :, 128 * ci:128 * ci + 128]), "dH%d" % ci, 16)
            w1dma[ci] = ctr.get("dH%d" % ci)

    for ci in range(32):
        @block.tensor
        def _(t, ci=ci):
            wait_all(t, but=("dL", "dG", "dH0", "dH1", "dW0", "dW1"))
            t.wait_ge(sems["dH%d" % (ci % 2)], w1dma[ci])
            lq = w1qct[ci % 2]
            for nb in range(4):
                pcol = 512 * nb
                for k in range(ND):
                    t.matmul(ps_a[:, pcol:pcol + 512], lq[:, 128 * k:128 * k + 128],
                             xq[:, 2048 * k + 512 * nb:2048 * k + 512 * nb + 512],
                             start=(k == 0), stop=(k == ND - 1))
            for k in range(ND):
                t.matmul(ps_b[:, 0:1], lq[:, 128 * k:128 * k + 128],
                         haloxq[:, k:k + 1], start=(k == 0), stop=(k == ND - 1))
            for k in range(ND):
                t.matmul(ps_b[:, 1:2], lq[:, 128 * k:128 * k + 128],
                         haloxq[:, 8 + k:8 + k + 1], start=(k == 0), stop=(k == ND - 1))
            semi(t, "tP")

        @block.vector
        def _(v, ci=ci):
            wait_all(v, but=("dL", "dG", "dH0", "dH1", "dW0", "dW1"))
            for nb in range(4):
                pcol = 512 * nb
                v.tensor_tensor(y1buf[:, 1 + 512 * nb:1 + 512 * nb + 512],
                                ps_a[:, pcol:pcol + 512], fxt[:, 512 * nb:512 * nb + 512],
                                ALU.mult)
            v.tensor_tensor(y1buf[:, 0:1], ps_b[:, 0:1], halofx[:, 0:1], ALU.mult)
            v.tensor_tensor(y1buf[:, 2049:2050], ps_b[:, 1:2], halofx[:, 1:2], ALU.mult)
            semi(v, "vP")

        if ci + 2 < 32:
            @block.sync
            def _(sy, ci=ci):
                sy.wait_ge(sems["tP"], ctr.get("tP"))
                inc(sy.dma_start(out=w1qct[ci % 2].rearrange("p (n c) -> p n c", n=ND),
                                 in_=w1q_sp[:, :, 128 * (ci + 2):128 * (ci + 2) + 128]),
                    "dH%d" % (ci % 2), 16)
                w1dma[ci + 2] = ctr.get("dH%d" % (ci % 2))

        @block.vector
        def _(v, ci=ci):
            wait_all(v, but=("dG", "dH0", "dH1", "dW0", "dW1"))
            v.tensor_scalar(htile[:, 0:2048], y1buf[:, 1:2049],
                            cst("convw")[:, 3 * ci + 1:3 * ci + 2],
                            cst("convb")[:, ci:ci + 1], ALU.mult, ALU.add)
            semi(v, "vP")

        @block.vector
        def _(v, ci=ci):
            wait_all(v, but=("dL", "dG", "dH0", "dH1", "dW0", "dW1"))
            v.scalar_tensor_tensor(htile[:, 0:2048], y1buf[:, 0:2048],
                                   cst("convw")[:, 3 * ci:3 * ci + 1],
                                   htile[:, 0:2048], ALU.mult, ALU.add)
            v.drain()
            v.scalar_tensor_tensor(htile[:, 0:2048], y1buf[:, 2:2050],
                                   cst("convw")[:, 3 * ci + 2:3 * ci + 3],
                                   htile[:, 0:2048], ALU.mult, ALU.add)
            semi(v, "vP")

        @block.scalar
        def _(s, ci=ci):
            wait_all(s, but=("dL", "dG", "dH0", "dH1", "dW0", "dW1"))
            s.activation(htile[:, 0:2048], htile[:, 0:2048], AF.Silu)
            semi(s, "sP")

        @block.sync
        def _(sy, ci=ci):
            wait_all(sy, but=("dG", "dH0", "dH1", "dW0", "dW1"))
            dseq(sy, hf_sp[:, ci, :], htile[:, 0:2048])

        @block.vector
        def _(v, ci=ci):
            wait_all(v, but=("dL", "dG", "dH0", "dH1", "dW0", "dW1"))
            if ci == 0:
                v.tensor_copy(runmax, htile[:, 0:2048])
                v.drain()
                v.tensor_copy(runmin, htile[:, 0:2048])
            else:
                v.tensor_tensor(runmax, runmax, htile[:, 0:2048], ALU.max)
                v.drain()
                v.tensor_tensor(runmin, runmin, htile[:, 0:2048], ALU.min)
            semi(v, "vP")

    @block.vector
    def _(v):
        wait_all(v, but=("dG",))
        v.tensor_scalar(runmin, runmin, -1.0, None, ALU.mult)
        v.drain()
        v.tensor_tensor(runmax, runmax, runmin, ALU.max)
        semi(v, "vP")

    @block.gpsimd
    def _(g):
        wait_all(g)
        g.tensor_reduce(runmax[0:1, :], runmax, AX.C, ALU.max)
        g.drain().then_inc(sems["gP"], 1)
    env["ctr"].add("gP", 1)

    @block.tensor
    def _(t):
        wait_all(t, but=("dG",))
        for nb in range(4):
            t.matmul(ps_a[:, 512 * nb:512 * nb + 512], cst("ones1"),
                     runmax[0:1, 512 * nb:512 * nb + 512], start=True, stop=True)
        semi(t, "tP")

    @block.vector
    def _(v):
        wait_all(v, but=("dG",))
        v.tensor_scalar(runmax, ps_a[:, 0:2048], 1e-5, None, ALU.max)
        v.drain()
        v.reciprocal(sxt, runmax)
        v.drain()
        v.tensor_scalar(sxt, sxt, 127.0, None, ALU.mult)
        v.drain()
        v.scalar_tensor_tensor(fxt, runmax, 1.0 / 127.0, mw2rep.broadcast_to([128, 2048]),
                               ALU.mult, ALU.mult)
        semi(v, "vP")

    w2dma = {}
    i8b = bufA[:, 2048:2176].bitcast(I8)
    hstB = bufA[:, 0:2048]
    w2qb = [bufA_b[:, 8192:12288], bufA_b[:, 12288:16384]]
    for tb in range(4):
        for ci in range(32):
            @block.sync
            def _(sy, tb=tb, ci=ci):
                par = "dH%d" % (ci % 2)
                if ci == 0 and tb == 0:
                    wait_all(sy, but=("dG",))
                else:
                    sy.wait_ge(sems["vP"], ctr.get("vP") - 3)
                    sy.wait_ge(sems[par], ctr.get(par))
                inc(sy.dma_start(out=hstB[:, 512 * (ci % 4):512 * (ci % 4) + 512],
                                 in_=hf_sp[:, ci, 512 * tb:512 * tb + 512]),
                    par, 16)
                w1dma[(tb, ci)] = (par, ctr.get(par))

            @block.vector
            def _(v, tb=tb, ci=ci):
                v.wait_ge(sems[w1dma[(tb, ci)][0]], w1dma[(tb, ci)][1])
                v.drain()
                v.tensor_tensor(bufA[:, 2176:2688], hstB[:, 512 * (ci % 4):512 * (ci % 4) + 512],
                                sxt[:, 512 * tb:512 * tb + 512], ALU.mult)
                v.drain()
                v.tensor_scalar(hq[:, 512 * ci:512 * ci + 512], bufA[:, 2176:2688],
                                MAGIC, -MAGIC, ALU.add, ALU.add)
                semi(v, "vP")

        for d2 in range(ND):
            @block.sync
            def _(sy, tb=tb, d2=d2):
                par = "dW%d" % (d2 % 2)
                if d2 >= 2:
                    sy.wait_ge(sems["tP"], ctr.get("tP") - 1)
                elif tb == 0 and d2 == 0:
                    wait_all(sy, but=("dG", "dH0", "dH1", "dW0", "dW1"))
                sy.wait_ge(sems[par], ctr.get(par))
                ap = w2q_sp[:, d2, :, :]
                inc(sy.dma_start(out=w2qb[d2 % 2].rearrange("p (a c) -> p a c", a=32), in_=ap),
                    par, 16)
                w2dma[(tb, d2)] = (par, ctr.get(par))

            @block.tensor
            def _(t, tb=tb, d2=d2):
                if d2 == 0:
                    wait_all(t, but=("dL", "dG", "dH0", "dH1", "dW0", "dW1"))
                t.wait_ge(sems[w2dma[(tb, d2)][0]], w2dma[(tb, d2)][1])
                if d2 >= 2:
                    t.wait_ge(sems["vP"], ctr.get("vP") - 1)
                wq2 = w2qb[d2 % 2]
                for ci in range(32):
                    t.matmul(ps_a[:, 512 * (d2 % 2):512 * (d2 % 2) + 512],
                             wq2[:, 128 * ci:128 * ci + 128],
                             hq[:, 512 * ci:512 * ci + 512],
                             start=(ci == 0), stop=(ci == 31))
                semi(t, "tP")

            @block.vector
            def _(v, tb=tb, d2=d2):
                v.wait_ge(sems["tP"], ctr.get("tP"))
                sl = slice(d2 * TC + 512 * tb, d2 * TC + 512 * tb + 512)
                v.drain()
                v.tensor_tensor(bufA[:, 2176:2688],
                                ps_a[:, 512 * (d2 % 2):512 * (d2 % 2) + 512],
                                fxt[:, 512 * tb:512 * tb + 512], ALU.mult)
                v.drain()
                v.tensor_tensor(x_res[:, sl], x_res[:, sl], bufA[:, 2176:2688], ALU.add)
                semi(v, "vP")

    emit_ln(env, lambda dd: x_res[:, dd * TC:(dd + 1) * TC],
            lambda dd: x_res[:, dd * TC:(dd + 1) * TC],
            bufD[0:1, 0:2048], bufC[0:1, 2048:4096], bufC[:, 0:2048], smll[:, 0:512])



def _phase3_l1attn(env):
    cst = env["cst"]; dseq = env["dseq"]
    nc = env["nc"]; block = env["block"]; inp = env["inp"]
    ctr = env["ctr"]; inc = env["inc"]; semi = env["semi"]; wait_all = env["wait_all"]
    x_res = env["x_res"]; bufA = env["bufA"]; bufB = env["bufB"]; bufC = env["bufC"]
    bufD = env["bufD"]; smll = env["smll"]; wgt = env["wgt"]
    ps_a = env["ps_a"]; ps_b = env["ps_b"]; ps_t = env["ps_t"]
    sems = env["sems"]

    bufA_b = bufA[:, :].bitcast(BF16)
    bufB_b = bufB[:, :].bitcast(BF16)
    bufC_b = bufC[:, :].bitcast(BF16)
    bufD_b = bufD[:, :].bitcast(BF16)
    wgt_b = wgt[:, :].bitcast(BF16)
    ps_tf = ps_t[:, :].bitcast(F32)
    ASCALE = 128.0 ** -0.5

    x2b = bufB_b[:, 0:16384]
    qT = bufA_b[:, 0:16384]
    wkd = wgt_b[:, 0:512]
    wvd = wgt_b[:, 512:1024]
    wku = wgt_b[:, 1024:1152]
    wvu = wgt_b[:, 1152:1280]
    awq_h = wgt_b[:, 4096:8192]
    klat_b = [wgt_b[:, 8192:10240], wgt_b[:, 10240:12288]]
    vlat_b = [wgt_b[:, 12288:14336], wgt_b[:, 14336:16384]]
    sqn = bufC[:, 0:2048]
    cells = smll[:, 0:128]

    @block.gpsimd
    def _(g):
        wait_all(g, but=("dL",))
        for dst, name in [(wkd, "a_wkd"), (wvd, "a_wvd")]:
            ap = inp[name].rearrange("(n p) c -> p n c", p=128)
            inc(g.dma_start(out=dst.rearrange("p (n c) -> p n c", n=ND), in_=ap), "dG", 16)
        inc(g.dma_start(out=wku[0:32, :], in_=inp["a_wku"]), "dG", 16)
        inc(g.dma_start(out=wvu[0:32, :], in_=inp["a_wvu"]), "dG", 16)

    for dd in range(ND):
        @block.scalar
        def _(s, dd=dd):
            if dd == 0:
                wait_all(s, but=("dG",))
            s.copy(x2b[:, 2048 * dd:2048 * dd + 2048], x_res[:, dd * TC:(dd + 1) * TC])
            semi(s, "sP")

    for half in range(2):
        @block.gpsimd
        def _(g, half=half):
            wait_all(g, but=("dL",))
            ap = inp["a_wq"].rearrange("(n p) c -> p n c", p=128)[:, :, 512 * half:512 * half + 512]
            inc(g.dma_start(out=awq_h.rearrange("p (n c) -> p n c", n=ND), in_=ap), "dG", 16)

        for hh in range(4):
            h = 4 * half + hh

            @block.tensor
            def _(t, half=half, hh=hh):
                wait_all(t, but=("dL",))
                for nb in range(4):
                    for k in range(ND):
                        t.matmul(ps_a[:, 512 * nb:512 * nb + 512],
                                 awq_h[:, 512 * k + 128 * hh:512 * k + 128 * hh + 128],
                                 x2b[:, 2048 * k + 512 * nb:2048 * k + 512 * nb + 512],
                                 start=(k == 0), stop=(k == ND - 1))
                semi(t, "tP")

            @block.vector
            def _(v, h=h):
                wait_all(v, but=("dL", "dG"))
                v.tensor_copy(qT[:, 2048 * h:2048 * h + 2048], ps_a[:, 0:2048])
                semi(v, "vP")

            @block.scalar
            def _(s, h=h):
                wait_all(s, but=("dL", "dG"))
                s.activation(sqn, ps_a[:, 0:2048], AF.Square)
                semi(s, "sP")

            @block.tensor
            def _(t, h=h):
                wait_all(t, but=("dL", "dG"))
                for nb in range(4):
                    t.matmul(ps_a[0:1, 512 * nb:512 * nb + 512], cst("onescol"),
                             sqn[:, 512 * nb:512 * nb + 512], start=True, stop=True)
                semi(t, "tP")

            @block.vector
            def _(v, h=h):
                wait_all(v, but=("dL", "dG"))
                v.tensor_reduce(cells[0:1, h:h + 1], ps_a[0:1, 0:2048], AX.X, ALU.max)
                semi(v, "vP")

    @block.sync
    def _(sy):
        wait_all(sy, but=("dG",))
        for h in range(8):
            dst = env["ag_q_in"].ap().rearrange("(h p j) c -> h p j c", h=8, p=128)[h]
            dseq(sy, dst, qT[:, 2048 * h:2048 * h + 2048].rearrange("p (j c) -> p j c", j=4))

    for kv in range(2):
        for wsb, dst in [(wkd, klat_b[kv]), (wvd, vlat_b[kv])]:
            @block.tensor
            def _(t, kv=kv, wsb=wsb):
                wait_all(t, but=("dL", "dG"))
                for nb in range(4):
                    for k in range(ND):
                        t.matmul(ps_a[0:32, 512 * nb:512 * nb + 512],
                                 wsb[:, 64 * k + 32 * kv:64 * k + 32 * kv + 32],
                                 x2b[:, 2048 * k + 512 * nb:2048 * k + 512 * nb + 512],
                                 start=(k == 0), stop=(k == ND - 1))
                semi(t, "tP")

            @block.vector
            def _(v, dst=dst):
                wait_all(v, but=("dL", "dG"))
                v.tensor_copy(dst[0:32, :], ps_a[0:32, 0:2048])
                semi(v, "vP")

    kTsb = [bufA_b[:, 0:2048], bufA_b[:, 2048:4096]]
    Vall = bufA_b[:, 4096:8192]

    for kv in range(2):
        @block.tensor
        def _(t, kv=kv):
            wait_all(t, but=("dL", "dG"))
            for nb in range(4):
                t.matmul(ps_a[:, 512 * nb:512 * nb + 512], wku[0:32, :],
                         klat_b[kv][0:32, 512 * nb:512 * nb + 512], start=True, stop=True)
            semi(t, "tP")

        @block.vector
        def _(v, kv=kv):
            wait_all(v, but=("dG",))
            v.tensor_copy(kTsb[kv], ps_a[:, 0:2048])
            if kv == 1:
                v.memset(cells[:, 10:16], 0.0)
            semi(v, "vP")

        @block.scalar
        def _(s, kv=kv):
            wait_all(s, but=("dL", "dG"))
            s.activation(sqn, ps_a[:, 0:2048], AF.Square)
            semi(s, "sP")

        @block.tensor
        def _(t, kv=kv):
            wait_all(t, but=("dL", "dG"))
            for nb in range(4):
                t.matmul(ps_a[0:1, 512 * nb:512 * nb + 512], cst("onescol"),
                         sqn[:, 512 * nb:512 * nb + 512], start=True, stop=True)
            semi(t, "tP")

        @block.vector
        def _(v, kv=kv):
            wait_all(v, but=("dL", "dG"))
            v.tensor_reduce(cells[0:1, 8 + kv:8 + kv + 1], ps_a[0:1, 0:2048], AX.X, ALU.max)
            semi(v, "vP")

    for kv in range(2):
        for grp in range(4):
            @block.tensor
            def _(t, kv=kv, grp=grp):
                wait_all(t, but=("dL", "dG"))
                for n4 in range(4):
                    n = 4 * grp + n4
                    t.matmul(ps_a[:, 512 * n4:512 * n4 + 128],
                             vlat_b[kv][0:32, 128 * n:128 * n + 128], wvu[0:32, :],
                             start=True, stop=True)
                semi(t, "tP")

            @block.vector
            def _(v, kv=kv, grp=grp):
                wait_all(v, but=("dL", "dG"))
                for n4 in range(4):
                    n = 4 * grp + n4
                    v.tensor_copy(Vall[:, 256 * n + 128 * kv:256 * n + 128 * kv + 128],
                                  ps_a[:, 512 * n4:512 * n4 + 128])
                semi(v, "vP")

    @block.sync
    def _(sy):
        wait_all(sy, but=("dG",))
        for kv in range(2):
            dseq(sy, env["ag_k_in"].ap()[128 * kv:128 * kv + 128, :], kTsb[kv])
        dseq(sy, env["ag_vin"].ap().rearrange("(n p) c -> p n c", p=128),
             Vall.rearrange("p (n c) -> p n c", n=16))
        dseq(sy, env["xh_in"].ap()[:, 0:16], cells[:, 0:16])

    @block.gpsimd
    def _(g):
        wait_all(g, but=("dI",))
        for iname, oname in [("ag_k_in", "ag_k_out"), ("ag_vin", "ag_vout"),
                             ("ag_q_in", "ag_q_out"), ("xh_in", "xh_out")]:
            inc(g.collective_compute(
                "AllGather", ALU.bypass, replica_groups=[[0, 1, 2, 3], [4, 5, 6, 7]],
                ins=[env[iname].ap().opt()], outs=[env[oname].ap().opt()],
            ), "gP", 1)

    @block.sync
    def _(sy):
        wait_all(sy, but=("dG",))
        for q in range(4):
            dseq(sy, smll[:, 128 + 16 * q:128 + 16 * q + 16],
                 env["xh_out"].ap()[128 * q:128 * q + 128, :])

    @block.vector
    def _(v):
        wait_all(v, but=("dG",))
        v.tensor_copy(cells[:, 0:16], smll[:, 128:144])
        for q in range(1, 4):
            v.drain()
            v.tensor_tensor(cells[:, 0:16], cells[:, 0:16],
                            smll[:, 128 + 16 * q:128 + 16 * q + 16], ALU.max)
        semi(v, "vP")

    @block.scalar
    def _(s):
        wait_all(s, but=("dG",))
        s.activation(cells[0:1, 16:26], cells[0:1, 0:10], AF.Ln)
        semi(s, "sP")

    @block.vector
    def _(v):
        wait_all(v, but=("dG",))
        for h in range(8):
            v.drain()
            v.tensor_tensor(cells[0:1, 32 + h:33 + h], cells[0:1, 16 + h:17 + h],
                            cells[0:1, 24 + h // 4:25 + h // 4], ALU.add)
        v.drain()
        v.tensor_scalar(cells[0:1, 32:40], cells[0:1, 32:40], 0.5, None, ALU.mult)
        semi(v, "vP")

    @block.scalar
    def _(s):
        wait_all(s, but=("dG",))
        s.activation(cells[0:1, 32:40], cells[0:1, 32:40], AF.Exp)
        semi(s, "sP")

    @block.vector
    def _(v):
        wait_all(v, but=("dG",))
        v.tensor_scalar(cells[0:1, 32:40], cells[0:1, 32:40], -ASCALE, None, ALU.mult)
        for h in range(8):
            for sl in range(5):
                v.drain()
                if sl == 0:
                    v.tensor_copy(cells[0:1, 40 + 5 * h:41 + 5 * h], cells[0:1, 32 + h:33 + h])
                else:
                    v.tensor_tensor(cells[0:1, 40 + 5 * h + sl:41 + 5 * h + sl],
                                    cells[0:1, 32 + h:33 + h],
                                    cst("attn_mb")[0:1, sl - 1:sl], ALU.add)
        semi(v, "vP")

    @block.tensor
    def _(t):
        wait_all(t, but=("dG",))
        t.matmul(ps_b[:, 0:40], cst("ones1"), cells[0:1, 40:80], start=True, stop=True)
        semi(t, "tP")

    biasT = smll[:, 80:120]

    @block.vector
    def _(v):
        wait_all(v, but=("dG",))
        v.tensor_copy(biasT, ps_b[:, 0:40])
        semi(v, "vP")

    idxq_sb = smll[:, 120:122].bitcast(I32)[:, 0:1]
    idxo_sb = smll[:, 120:122].bitcast(I32)[:, 1:2]

    @block.sync
    def _(sy):
        wait_all(sy, but=("dG",))
        dseq(sy, bufB_b[:, 0:8192], inp["amask"])
        dseq(sy, idxq_sb, inp["aidxq"])
        dseq(sy, idxo_sb, inp["aidxo"])

    qg = bufD_b[:, 0:4096]
    Kg = [bufC_b[:, 0:1024], bufC_b[:, 1024:2048]]
    Vg = [bufC_b[:, 2048:3072], bufC_b[:, 3072:4096]]
    Pt = [bufC_b[:, 4096:4608], bufC_b[:, 4608:5120]]
    oex = bufA_b[:, 0:4096]
    amask = bufB_b[:, 0:8192]
    lrep = bufB[:, 4096:6144]
    rsb = bufB[:, 6144:6656]
    l3row = bufB[0:1, 6656:7168]
    Sbank = [ps_b[:, 0:512], ps_b[:, 512:1024]]
    kvdma = {}

    def kv_fetch(sy, i, buf):
        par = "dW%d" % buf
        v0 = ctr.get(par)
        if v0 > 0:
            sy.wait_ge(sems[par], v0)
        o = i // 4
        c0 = 512 * (i % 4)
        for kv in range(2):
            ctr.add(par, 16)
            sy.dma_start(out=Kg[buf][:, 512 * kv:512 * kv + 512],
                         in_=env["ag_k_out"].ap()[256 * o + 128 * kv:256 * o + 128 * kv + 128,
                                                  c0:c0 + 512]).then_inc(sems[par], 16)
        ctr.add(par, 16)
        sy.dma_start(out=Vg[buf].rearrange("p (s c) -> p s c", s=4),
                     in_=env["ag_vout"].ap()[512 * i:512 * i + 512, :]
                     .rearrange("(s p) c -> p s c", p=128)).then_inc(sems[par], 16)
        kvdma[i] = (par, ctr.get(par))

    for kg in range(4):
        NI = 4 * kg + 4

        @block.gpsimd
        def _(g, kg=kg):
            wait_all(g, but=("dW0", "dW1", "sP") + (("dL",) if kg else ()))
            for h in range(8):
                inc(g.indirect_dma_start(
                    out=qg[:, 512 * h:512 * h + 512], out_offset=None,
                    in_=env["ag_q_out"].ap(),
                    in_offset=bass.IndirectOffsetOnAxis(idxq_sb, axis=0),
                    element_offset=(4096 * kg + 512 * h) * 512,
                ), "dI", 16)

        @block.sync
        def _(sy, kg=kg):
            wait_all(sy, but=("dG", "dI", "dL", "sP"))
            kv_fetch(sy, 0, 0)
            kv_fetch(sy, 1, 1)

        for st in range(2):
            J = NI * 16
            if st == 1:
                @block.sync
                def _(sy, kg=kg):
                    sy.wait_ge(sems["tP"], ctr.get("tP"))
                    kv_fetch(sy, 0, 0)
                    kv_fetch(sy, 1, 1)

            t0 = ctr.get("tP")
            s0 = ctr.get("sP")
            v0 = ctr.get("vP")
            masked = {j: (j // 16 >= 4 * kg) for j in range(J)}

            @block.sync
            def _(sy, kg=kg, NI=NI, t0=t0):
                for i in range(2, NI):
                    sy.wait_ge(sems["tP"], t0 + 16 * (i - 1) + 2)
                    kv_fetch(sy, i, i % 2)

            @block.tensor
            def _(t, kg=kg, st=st, NI=NI, J=J, t0=t0, s0=s0, v0=v0):
                wait_all(t, but=("dW0", "dW1"))
                nmask_done = 0
                for j in range(J + 2):
                    if j < J:
                        i, r = divmod(j, 16)
                        s, hh = divmod(r, 4)
                        h = 4 * st + hh
                        if r == 0:
                            t.wait_ge(sems[kvdma[i][0]], kvdma[i][1])
                    if j >= 2:
                        jj = j - 2
                        i2, r2 = divmod(jj, 16)
                        s2, hh2 = divmod(r2, 4)
                        h2 = 4 * st + hh2
                        if masked[jj]:
                            nmask_done += 1
                            t.wait_ge(sems["vP"], v0 + nmask_done)
                        else:
                            t.wait_ge(sems["sP"], s0 + jj + 1)
                        if hh2 == 0:
                            t.matmul(ps_b[0:1, 1024:1536], cst("onescol_b"), Pt[jj % 2],
                                     start=(i2 == 0 and s2 == 0),
                                     stop=(i2 == NI - 1 and s2 == 3))
                        elif hh2 == 3:
                            t.matmul(ps_tf[0:1, 0:512], cst("onescol_b"), Pt[jj % 2],
                                     start=(i2 == 0 and s2 == 0),
                                     stop=(i2 == NI - 1 and s2 == 3))
                        else:
                            t.matmul(ps_b[32 * hh2:32 * hh2 + 1, 1024:1536],
                                     cst("onescol_b"), Pt[jj % 2],
                                     start=(i2 == 0 and s2 == 0),
                                     stop=(i2 == NI - 1 and s2 == 3),
                                     tile_position=(0, 32 * hh2))
                        last = t.matmul(
                            ps_a[:, 512 * hh2:512 * hh2 + 512],
                            Vg[i2 % 2][:, 256 * s2 + 128 * (h2 // 4):256 * s2 + 128 * (h2 // 4) + 128],
                            Pt[jj % 2], start=(i2 == 0 and s2 == 0), stop=(i2 == NI - 1 and s2 == 3))
                    if j < J:
                        last = t.matmul(
                            Sbank[j % 2],
                            Kg[i % 2][:, 512 * st + 128 * s:512 * st + 128 * s + 128],
                            qg[:, 512 * h:512 * h + 512], start=True, stop=True)
                    inc(last, "tP")

            @block.scalar
            def _(s_, kg=kg, st=st, J=J, t0=t0):
                for j in range(J):
                    i, r = divmod(j, 16)
                    sblk, hh = divmod(r, 4)
                    h = 4 * st + hh
                    slot = 0 if i < 4 * kg else 1 + (i - 4 * kg)
                    s_.wait_ge(sems["tP"], t0 + j + 1)
                    inc(s_.activation(Pt[j % 2], Sbank[j % 2], AF.Exp,
                                      bias=biasT[:, 5 * h + slot:5 * h + slot + 1],
                                      scale=ASCALE), "sP")

            @block.vector
            def _(v, kg=kg, J=J, s0=s0):
                for j in range(J):
                    if not masked[j]:
                        continue
                    i, r = divmod(j, 16)
                    sblk, hh = divmod(r, 4)
                    sl = i - 4 * kg
                    v.wait_ge(sems["sP"], s0 + j + 1)
                    inc(v.tensor_tensor(
                        Pt[j % 2], Pt[j % 2],
                        amask[:, 2048 * sl + 512 * sblk:2048 * sl + 512 * sblk + 512],
                        ALU.mult), "vP")

            @block.vector
            def _(v):
                wait_all(v, but=("dW0", "dW1"))
                v.tensor_copy(rsb, ps_b[:, 1024:1536])
                v.tensor_copy(l3row, ps_tf[0:1, 0:512])
                semi(v, "vP")

            for hh in range(4):
                @block.tensor
                def _(t, hh=hh):
                    wait_all(t, but=("dW0", "dW1"))
                    if hh == 3:
                        t.matmul(ps_b[:, 0:512], cst("ones1"), l3row,
                                 start=True, stop=True)
                    else:
                        t.matmul(ps_b[:, 0:512], cst("onesf")[32 * hh:32 * hh + 1, :],
                                 rsb[32 * hh:32 * hh + 1, :], start=True, stop=True)
                    semi(t, "tP")

                @block.vector
                def _(v, hh=hh):
                    wait_all(v, but=("dW0", "dW1"))
                    v.tensor_copy(lrep[:, 512 * hh:512 * hh + 512], ps_b[:, 0:512])
                    semi(v, "vP")

            @block.scalar
            def _(s):
                wait_all(s, but=("dW0", "dW1"))
                s.activation(lrep, lrep, AF.Ln)
                s.drain()
                s.activation(lrep, lrep, AF.Exp, scale=-1.0)
                semi(s, "sP")

            @block.vector
            def _(v, st=st):
                wait_all(v, but=("dW0", "dW1"))
                for hh in range(4):
                    v.tensor_tensor(oex[:, 512 * (4 * st + hh):512 * (4 * st + hh) + 512],
                                    ps_a[:, 512 * hh:512 * hh + 512],
                                    lrep[:, 512 * hh:512 * hh + 512], ALU.mult)
                semi(v, "vP")

        @block.sync
        def _(sy, kg=kg):
            wait_all(sy, but=("dG", "dI"))
            dseq(sy, env["ag_o_in"].ap().rearrange("(k h p) c -> k p h c", h=8, p=128)[kg],
                 oex.rearrange("p (h c) -> p h c", h=8))

    @block.gpsimd
    def _(g):
        wait_all(g)
        inc(g.collective_compute(
            "AllGather", ALU.bypass, replica_groups=[[0, 1, 2, 3], [4, 5, 6, 7]],
            ins=[env["ag_o_in"].ap().opt()], outs=[env["ag_o_out"].ap().opt()],
        ), "gP", 1)

    oT = bufB_b[:, 0:16384]

    @block.gpsimd
    def _(g):
        wait_all(g)
        for lg in range(4):
            for h in range(8):
                inc(g.indirect_dma_start(
                    out=oT[:, 2048 * h + 512 * lg:2048 * h + 512 * lg + 512],
                    out_offset=None,
                    in_=env["ag_o_out"].ap(),
                    in_offset=bass.IndirectOffsetOnAxis(idxo_sb, axis=0),
                    element_offset=(4096 * lg + 128 * h) * 512,
                ), "dI", 16)
        _, ap, d = _wload(env, wgt_b[:, 0:8192], "a_wo")
        inc(g.dma_start(out=d, in_=ap), "dG", 16)

    v_base = ctr.get("vP")
    for dt in range(ND):
        @block.tensor
        def _(t, dt=dt):
            if dt == 0:
                wait_all(t)
            else:
                t.wait_ge(sems["vP"], v_base + dt)
            for nb in range(4):
                for k in range(ND):
                    t.matmul(ps_a[:, 512 * nb:512 * nb + 512],
                             wgt_b[:, 1024 * k + 128 * dt:1024 * k + 128 * dt + 128],
                             oT[:, 2048 * k + 512 * nb:2048 * k + 512 * nb + 512],
                             start=(k == 0), stop=(k == ND - 1))
            semi(t, "tP")

        @block.vector
        def _(v, dt=dt):
            wait_all(v, but=("dL", "dG"))
            for nb in range(4):
                v.tensor_tensor(x_res[:, dt * TC + 512 * nb:dt * TC + 512 * nb + 512],
                                x_res[:, dt * TC + 512 * nb:dt * TC + 512 * nb + 512],
                                ps_a[:, 512 * nb:512 * nb + 512], ALU.add)
            semi(v, "vP")

    emit_ln(env, lambda dd: x_res[:, dd * TC:(dd + 1) * TC],
            lambda dd: x_res[:, dd * TC:(dd + 1) * TC],
            bufD[0:1, 0:2048], bufC[0:1, 2048:4096], bufC[:, 0:2048], smll[:, 0:512])



def _phase4_l1ffn(env):
    cst = env["cst"]; dseq = env["dseq"]
    nc = env["nc"]; block = env["block"]; inp = env["inp"]
    ctr = env["ctr"]; inc = env["inc"]; semi = env["semi"]; wait_all = env["wait_all"]
    x_res = env["x_res"]; bufA = env["bufA"]; bufB = env["bufB"]; bufC = env["bufC"]
    bufD = env["bufD"]; smll = env["smll"]; wgt = env["wgt"]
    ps_a = env["ps_a"]; ps_b = env["ps_b"]; ps_t = env["ps_t"]
    sems = env["sems"]
    innT_sp = env["innT_sp"]

    bufA_b = bufA[:, :].bitcast(BF16)
    bufB_b = bufB[:, :].bitcast(BF16)
    bufC_b = bufC[:, :].bitcast(BF16)
    bufD_b = bufD[:, :].bitcast(BF16)
    wgt_b = wgt[:, :].bitcast(BF16)
    ps_tf = ps_t[:, :].bitcast(F32)

    x3b = bufA_b[:, 0:16384]
    wc_sb = wgt[:, 0:1024]
    msel = wgt_b[:, 8192:12288]
    e_b2fb = wgt_b[:, 16384:17408]
    w1t = [wgt_b[:, 2048:3072], wgt_b[:, 3072:4096]]
    lgT = bufD[:, 0:2048]
    lgT3 = bufD[:, 0:2048].rearrange("p (g t) -> p g t", g=64)
    gmax = smll[:, 0:64]
    pm = bufC_b[:, 0:2048]
    npm = bufC_b[:, 2048:4096]
    pmB = bufD_b[:, 0:2048]
    innb = bufC_b[:, 6144:8192]

    for dd in range(ND):
        @block.scalar
        def _(s, dd=dd):
            if dd == 0:
                wait_all(s, but=("dG", "dI"))
            s.copy(x3b[:, 2048 * dd:2048 * dd + 2048], x_res[:, dd * TC:(dd + 1) * TC])
            semi(s, "sP")

    @block.sync
    def _(sy):
        wait_all(sy, but=("dG", "dI"))
        dseq(sy, wc_sb.rearrange("p (n c) -> p n c", n=ND),
             inp["e_Wc"].rearrange("(n p) c -> p n c", p=128))
        dseq(sy, msel, inp["e_msel"])

    @block.gpsimd
    def _(g):
        wait_all(g)
        inc(g.dma_start(out=e_b2fb, in_=inp["e_b2f"]), "dG", 16)
        ap = inp["e_W2f"].rearrange("(n p) c -> p n c", p=128)[:, 16:32, :]
        inc(g.dma_start(out=bufB_b.rearrange("p (n c) -> p n c", n=16), in_=ap),
            "dG", 16)

    @block.tensor
    def _(t):
        wait_all(t, but=("dG", "dI"))
        for nb in range(4):
            for k in range(ND):
                t.matmul(ps_a[:, 512 * nb:512 * nb + 512],
                         wc_sb[:, 128 * k:128 * k + 128],
                         x_res[:, 2048 * k + 512 * nb:2048 * k + 512 * nb + 512],
                         start=(k == 0), stop=(k == ND - 1))
        semi(t, "tP")

    @block.vector
    def _(v):
        wait_all(v, but=("dG", "dI"))
        v.tensor_scalar(lgT, ps_a[:, 0:2048], cst("e_bc"), None, ALU.add)
        v.drain()
        v.tensor_tensor(lgT3, lgT3,
                        cst("tie128").unsqueeze(1).broadcast_to([128, 64, 32]), ALU.add)
        v.drain()
        v.tensor_reduce(gmax, lgT3, AX.X, ALU.max)
        v.drain()
        v.tensor_tensor(pm.rearrange("p (g t) -> p g t", g=64), lgT3,
                        gmax.unsqueeze(2).broadcast_to([128, 64, 32]), ALU.is_equal)
        v.tensor_tensor(npm.rearrange("p (g t) -> p g t", g=64), lgT3,
                        gmax.unsqueeze(2).broadcast_to([128, 64, 32]), ALU.not_equal)
        v.drain()
        v.tensor_copy(pmB, pm)
        semi(v, "vP")

    w1dma = {}

    def w1_fetch(g, mt):
        par = "dA%d" % (mt % 2)
        v0 = ctr.get(par)
        if v0 > 0:
            g.wait_ge(sems[par], v0)
        ap = inp["e_W1f"].rearrange("(n p) c -> p n c", p=128)[:, :, 128 * mt:128 * mt + 128]
        ctr.add(par, 16)
        g.dma_start(out=w1t[mt % 2].rearrange("p (n c) -> p n c", n=ND),
                    in_=ap).then_inc(sems[par], 16)
        w1dma[mt] = (par, ctr.get(par))

    @block.gpsimd
    def _(g):
        wait_all(g, but=("dL", "dI"))
        w1_fetch(g, 0)

    def psumset(mt):
        if mt % 2 == 0:
            return [ps_a[:, 512 * nb:512 * nb + 512] for nb in range(4)]
        return [ps_b[:, 0:512], ps_b[:, 512:1024], ps_b[:, 1024:1536], ps_tf[:, 0:512]]

    s_base = ctr.get("sP")
    for mt in range(32):
        if mt + 1 < 32:
            @block.gpsimd
            def _(g, mt=mt):
                g.wait_ge(sems["tP"], ctr.get("tP"))
                w1_fetch(g, mt + 1)

        @block.tensor
        def _(t, mt=mt):
            if mt == 0:
                wait_all(t, but=("dL", "dA0", "dA1", "dI", "sP"))
                t.wait_ge(sems["sP"], s_base)
            else:
                t.wait_ge(sems["sP"], s_base + max(0, mt - 1))
            t.wait_ge(sems[w1dma[mt][0]], w1dma[mt][1])
            ps = psumset(mt)
            for nb in range(4):
                for k in range(ND):
                    t.matmul(ps[nb], w1t[mt % 2][:, 128 * k:128 * k + 128],
                             x3b[:, 2048 * k + 512 * nb:2048 * k + 512 * nb + 512],
                             start=(k == 0), stop=False)
                t.matmul(ps[nb], msel[:, 128 * mt:128 * mt + 128],
                         npm[:, 512 * nb:512 * nb + 512], start=False, stop=True)
            semi(t, "tP")

        @block.scalar
        def _(s, mt=mt):
            wait_all(s, but=("dA0", "dA1", "dI"))
            if mt % 2 == 0:
                s.activation(innb, ps_a[:, 0:2048], AF.Relu,
                             bias=cst("e_b1f")[:, mt:mt + 1])
            else:
                s.activation(innb[:, 0:1536], ps_b[:, 0:1536], AF.Relu,
                             bias=cst("e_b1f")[:, mt:mt + 1])
                s.drain()
                s.activation(innb[:, 1536:2048], ps_tf[:, 0:512], AF.Relu,
                             bias=cst("e_b1f")[:, mt:mt + 1])
            semi(s, "sP")

        @block.sync
        def _(sy, mt=mt):
            wait_all(sy, but=("dA0", "dA1", "dI"))
            dseq(sy, innT_sp[:, mt, :], innb)

    @block.gpsimd
    def _(g):
        wait_all(g, but=("dI",))
        ap = inp["e_W2f"].rearrange("(n p) c -> p n c", p=128)[:, 0:16, :]
        inc(g.dma_start(out=bufA_b.rearrange("p (n c) -> p n c", n=16), in_=ap),
            "dG", 16)

    innhi = wgt_b[:, 8192:16384]

    def w2sb(mt):
        return (bufA_b if mt < 16 else bufB_b)[:, 1024 * (mt % 16):1024 * (mt % 16) + 1024]

    def innf(mt):
        return (bufC_b if mt < 16 else innhi)[:, 512 * (mt % 16):512 * (mt % 16) + 512]

    def psdt(dt):
        if dt < 4:
            return ps_a[:, 512 * dt:512 * dt + 512]
        if dt < 7:
            return ps_b[:, 512 * (dt - 4):512 * (dt - 4) + 512]
        return ps_tf[:, 0:512]

    for nb in range(4):
        @block.sync
        def _(sy, nb=nb):
            wait_all(sy, but=("dA0", "dA1", "dI"))
            dseq(sy, bufC_b.rearrange("p (n c) -> p n c", n=16),
                 innT_sp[:, 0:16, 512 * nb:512 * nb + 512])
            dseq(sy, innhi.rearrange("p (n c) -> p n c", n=16),
                 innT_sp[:, 16:32, 512 * nb:512 * nb + 512])

        @block.tensor
        def _(t, nb=nb):
            wait_all(t, but=("dA0", "dA1", "dI"))
            for mt in range(32):
                for dt in range(ND):
                    t.matmul(psdt(dt), w2sb(mt)[:, 128 * dt:128 * dt + 128],
                             innf(mt), start=(mt == 0), stop=False)
            for dt in range(ND):
                t.matmul(psdt(dt), e_b2fb[:, 128 * dt:128 * dt + 128],
                         pmB[:, 512 * nb:512 * nb + 512], start=False, stop=True)
            semi(t, "tP")

        @block.vector
        def _(v, nb=nb):
            wait_all(v, but=("dA0", "dA1", "dI"))
            for dt in range(ND):
                v.tensor_tensor(x_res[:, dt * TC + 512 * nb:dt * TC + 512 * nb + 512],
                                x_res[:, dt * TC + 512 * nb:dt * TC + 512 * nb + 512],
                                psdt(dt), ALU.add)
            semi(v, "vP")

    emit_ln(env, lambda dd: x_res[:, dd * TC:(dd + 1) * TC],
            lambda dd: x_res[:, dd * TC:(dd + 1) * TC],
            bufD[0:1, 0:2048], bufC[0:1, 2048:4096], bufC[:, 0:2048], smll[:, 0:512])


def _phase_out(env):
    nc = env["nc"]; block = env["block"]; out = env["out"]; dseq = env["dseq"]
    ctr = env["ctr"]; inc = env["inc"]; wait = env["wait"]; semi = env["semi"]
    wait_all = env["wait_all"]; sems = env["sems"]
    x_res = env["x_res"]; ps_a = env["ps_a"]; ps_b = env["ps_b"]; bufA = env["bufA"]

    t_base = ctr.get("tP")
    v_base = ctr.get("vP")
    dl_after = {}
    for tt in range(NT):
        pp = ps_b[:, 0:1024] if tt % 2 == 0 else ps_a[:, 0:1024]
        half = bufA[:, 1024 * (tt % 2):1024 * (tt % 2) + 1024]

        @block.tensor
        def _(t, tt=tt, pp=pp):
            if tt == 0:
                wait_all(t, but=("dG",))
            elif tt >= 2:
                t.wait_ge(sems["vP"], v_base + tt - 1)
            for dd in range(ND):
                t.transpose(pp[:, 128 * dd:128 * dd + 128],
                            x_res[:, dd * TC + 128 * tt: dd * TC + 128 * tt + 128],
                            env["cst"]("identf"))
            semi(t, "tP")

        @block.vector
        def _(v, tt=tt, pp=pp, half=half):
            v.wait_ge(sems["tP"], t_base + tt + 1)
            if tt >= 2:
                v.wait_ge(sems["dL"], dl_after[tt - 2])
            inc(v.tensor_copy(half, pp), "vP")

        @block.sync
        def _(sy, tt=tt, half=half):
            sy.wait_ge(sems["vP"], v_base + tt + 1)
            dseq(sy, out[128 * tt:128 * tt + 128, :], half)
            dl_after[tt] = ctr.get("dL")

    @block.sync
    def _(sy):
        wait_all(sy)



def kernel(**inputs):
    w = _prep_shared(inputs)
    consts = _prep_consts()
    x = np.ascontiguousarray(inputs["L0_attn_x"])
    in_maps = []
    packed_names = set(CSTF_MAP) | set(CSTB_MAP)
    for core in range(NCORE):
        b, p = divmod(core, 4)
        vals = dict(w)
        vals.update(consts)
        vals.update(_prep_percore(core))
        cf, cb = _pack_consts(vals)
        m = {k2: v for k2, v in vals.items() if k2 not in packed_names}
        m["cstf_all"] = cf
        m["cstb_all"] = cb
        m["xT"] = np.ascontiguousarray(x[b, 2048 * p:2048 * (p + 1), :].T)
        in_maps.append(m)
    with allow_ici_all_to_all():
        nc = build_nc()
        res = run_bass_kernel_spmd(nc, in_maps, core_ids=list(range(NCORE)))
    outp = np.empty((B, T, D), np.float32)
    for core in range(NCORE):
        b, p = divmod(core, 4)
        outp[b, 2048 * p:2048 * (p + 1), :] = res.results[core]["out"]
    return outp


if __name__ == "__main__":
    pass
